# version 02 (final)
# baseline (speedup 1.0000x reference)
import math
import numpy as np
from contextlib import ExitStack

import concourse.bass as bass
import concourse.mybir as mybir
from concourse.bass_utils import run_bass_kernel_spmd
from concourse.alu_op_type import AluOpType as ALU

F32 = mybir.dt.float32
BF16 = mybir.dt.bfloat16
I32 = mybir.dt.int32
AF = mybir.ActivationFunctionType
AX = mybir.AxisListType

N_CORES = 8
B, S, D = 4, 4096, 1024
S_LOC = S // N_CORES
T = B * S_LOC
NCH = D // 128
NTC = T // 512
EPS = 1e-5

ENGINES = ("tensor", "vector", "scalar", "gpsimd", "sync")


def _iv(ap, bank_granular=False):
    t = ap.tensor
    shape = list(t.shape)
    F = shape[-1] if len(shape) > 1 else shape[0]
    if len(shape) == 1:
        F = shape[0]
    off = ap.offset
    dims = [(int(s), int(c)) for (s, c) in ap.ap]
    p0 = off // F
    c0 = off % F
    pspan = 1
    cspan = 0
    for (s, c) in dims:
        if c <= 1:
            continue
        if s % F == 0 and s > 0:
            pspan += (s // F) * (c - 1)
        else:
            cspan += s * (c - 1)
    return (t.name, p0, p0 + pspan, c0, c0 + cspan + 1)


class Ctx:

    NDMA = 4

    def __init__(self, nc, es):
        self.nc = nc
        self.es = es
        self.ops = {e: [] for e in ENGINES}
        self.logs = {}
        self.psum_names = set()
        self.sems = {}
        self.final_waits = []
        self.dma_idx = {e: 0 for e in ENGINES}
        self.last_slot_op = {}

    def _log(self, name):
        if name not in self.logs:
            self.logs[name] = {"w": [], "r": []}
        return self.logs[name]

    def _expand(self, ap):
        name, p0, p1, c0, c1 = _iv(ap)
        if name in self.psum_names:
            c0 = (c0 // 512) * 512
            c1 = ((c1 + 511) // 512) * 512
            p0, p1 = 0, 128
        return (name, p0, p1, c0, c1)

    @staticmethod
    def _overlap(a, b):
        return a[1] < b[2] and b[1] < a[2] and a[3] < b[4] and b[3] < a[4]

    def _dep_merge(self, deps, entry_deps):
        for k, v in entry_deps.items():
            if deps.get(k, -1) < v:
                deps[k] = v

    def op(self, eng, fn, reads=(), writes=(), stream="c",
           war_writes=(), provide=True):
        pos = len(self.ops[eng])
        deps = {}
        rivs, wivs = [], []
        for ap in reads:
            iv = ap if isinstance(ap, tuple) else self._expand(ap)
            rivs.append(iv)
            lg = self._log(iv[0])
            for e in lg["w"]:
                if self._overlap(iv, e[0]):
                    self._dep_merge(deps, e[1])
        for ap in tuple(writes) + tuple(war_writes):
            iv = ap if isinstance(ap, tuple) else self._expand(ap)
            wivs.append(iv)
            lg = self._log(iv[0])
            for e in lg["w"]:
                if self._overlap(iv, e[0]):
                    self._dep_merge(deps, e[1])
            for e in lg["r"]:
                if self._overlap(iv, e[0]):
                    self._dep_merge(deps, e[1])
        if stream != "c":
            prev = self.last_slot_op.get((eng, stream))
            if prev is not None and deps.get((eng, stream), -1) < prev:
                deps[(eng, stream)] = prev
            self.last_slot_op[(eng, stream)] = pos
        deps.pop((eng, stream, pos), None)
        for iv in rivs:
            self._record("r", iv, eng, stream, pos)
        if provide:
            nw = len(war_writes)
            for iv in (wivs[:len(wivs) - nw] if nw else wivs):
                self._record("w", iv, eng, stream, pos)
        self.ops[eng].append([fn, deps, stream])
        return (eng, stream, pos)

    def record_write(self, ap, opid):
        eng, stream, pos = opid
        iv = ap if isinstance(ap, tuple) else self._expand(ap)
        self._record("w", iv, eng, stream, pos)

    def _record(self, kind, iv, eng, stream, pos):
        lg = self._log(iv[0])
        lst = lg[kind]
        key = (eng, stream)
        for e in lst:
            if e[0] == iv:
                if e[1].get(key, -1) < pos:
                    e[1][key] = pos
                break
        else:
            lst.append((iv, {key: pos}))
        if len(lst) > 96:
            old = lst[:48]
            box = list(old[0][0])
            dd = {}
            for e in old:
                b = e[0]
                box[1] = min(box[1], b[1]); box[2] = max(box[2], b[2])
                box[3] = min(box[3], b[3]); box[4] = max(box[4], b[4])
                self._dep_merge(dd, e[1])
            lg[kind] = [(tuple(box), dd)] + lst[48:]

    def dma(self, eng, out, in_, **kw):
        slot = "d%d" % (self.dma_idx[eng] % self.NDMA)
        self.dma_idx[eng] += 1
        return self.op(eng, (lambda e, o=out, i=in_, k=kw: e.dma_start(out=o, in_=i, **k)),
                       reads=[in_], writes=[out], stream=slot)

    def barrier_output(self, opid):
        self.final_waits.append(opid)

    def sbuf(self, name, shape, dtype):
        return self.es.enter_context(self.nc.sbuf_tensor(name, shape, dtype))

    def psum(self, name, shape, dtype=F32):
        h = self.es.enter_context(self.nc.psum_tensor(name, shape, dtype))
        self.psum_names.add(name)
        return h

    def dram(self, name, shape, dtype, kind="Internal", addr_space=None):
        if addr_space:
            t = self.nc.dram_tensor(name, shape, dtype, kind=kind, addr_space=addr_space)
        else:
            t = self.nc.dram_tensor(name, shape, dtype, kind=kind)
        return t.ap()

    def finalize(self):
        nc = self.nc
        needed = set()
        for eng in ENGINES:
            for rec in self.ops[eng]:
                for (deng, dstream), dpos in rec[1].items():
                    needed.add((deng, dstream, dpos))
        for opid in self.final_waits:
            needed.add(opid)
        incnum = {}
        counters = {}
        for eng in ENGINES:
            for pos, rec in enumerate(self.ops[eng]):
                stream = rec[2]
                if (eng, stream, pos) in needed:
                    key = (eng, stream)
                    step = 16 if stream != "c" else 1
                    counters[key] = counters.get(key, 0) + step
                    incnum[(eng, stream, pos)] = counters[key]
        for (eng, stream, pos) in incnum:
            key = (eng, stream)
            if key not in self.sems:
                self.sems[key] = self.es.enter_context(
                    nc.semaphore("s_%s_%s" % (eng, stream)))
        with nc.Block() as block:
            for eng_name in ENGINES:
                items = self.ops[eng_name]
                if not items:
                    continue

                def body(engine, eng_name=eng_name, items=items):
                    observed = {}
                    for pos, rec in enumerate(items):
                        fn, deps, stream = rec
                        for (deng, dstream), dpos in sorted(deps.items()):
                            key = (deng, dstream)
                            v = incnum.get((deng, dstream, dpos))
                            if v is None:
                                raise RuntimeError(
                                    f"dep on op without inc: {deng} {dstream} {dpos}")
                            if observed.get(key, -1) >= v:
                                continue
                            engine.wait_ge(self.sems[key], v)
                            observed[key] = v
                        instr = fn(engine)
                        v = incnum.get((eng_name, stream, pos))
                        if v is not None:
                            instr.then_inc(self.sems[(eng_name, stream)],
                                           16 if stream != "c" else 1)
                    for (feng, fstream, fpos) in self.final_waits:
                        if feng == eng_name:
                            engine.wait_ge(self.sems[(feng, fstream)],
                                           incnum[(feng, fstream, fpos)])

                getattr(block, eng_name)(body)


def mm_group(ctx, psum_ap, parts, extra_reads_first=()):
    n = len(parts)
    last = None
    for k, (lh, rh) in enumerate(parts):
        if k == 0:
            last = ctx.op("tensor",
                          (lambda e, lh=lh, rh=rh, st=(n == 1): e.matmul(
                              psum_ap, lh, rh, start=True, stop=st)),
                          reads=[lh, rh] + list(extra_reads_first),
                          war_writes=[psum_ap], provide=False)
        else:
            last = ctx.op("tensor",
                          (lambda e, lh=lh, rh=rh, st=(k == n - 1): e.matmul(
                              psum_ap, lh, rh, start=False, stop=st)),
                          reads=[lh, rh])
    ctx.record_write(psum_ap, last)
    return last


def make_common(ctx, name="", rows_tokens=T, arena_b=2048):
    sc = {}
    sc["arenaA"] = ctx.psum("psA" + name, [128, 2048], F32)
    sc["arenaB"] = ctx.psum("psB" + name, [128, arena_b], F32)
    sc["ones_col"] = ctx.sbuf("ones_col" + name, [128, 1], F32)
    sc["ones_colb"] = ctx.sbuf("ones_colb" + name, [128, 1], BF16)
    sc["ones_row"] = ctx.sbuf("ones_row" + name, [1, 512], F32)
    sc["ones_rowb"] = ctx.sbuf("ones_rowb" + name, [1, 512], BF16)
    sc["rows"] = ctx.sbuf("rows" + name, [1, 3 * rows_tokens], F32)
    sc["rows_tokens"] = rows_tokens
    sc["apt"] = ctx.sbuf("apt" + name, [128, 512], F32)
    for t in ("ones_col", "ones_colb", "ones_row", "ones_rowb"):
        ctx.op("vector", lambda e, t=t: e.memset(sc[t][:, :], 1.0),
               writes=[sc[t][:, :]])
    return sc


def emit_ln_stats(ctx, res, sc, sqscratch, nch=NCH, tokens=T, sq_slots=None, ones_key="ones_col", pre_tc=None):
    A = sc["arenaA"]
    Bb = sc["arenaB"]
    rows = sc["rows"]
    RT = sc["rows_tokens"]
    mean = rows[:, 0 * RT:0 * RT + tokens]
    b_row = rows[:, 0 * RT:0 * RT + tokens]
    tmp = rows[:, 1 * RT:1 * RT + tokens]
    a_row = rows[:, 2 * RT:2 * RT + tokens]
    ntc = tokens // 512
    if sq_slots is None:
        sq_slots = nch
    if not callable(res):
        res_t = res
        res = lambda k, tc: res_t[:, k * tokens + tc * 512: k * tokens + tc * 512 + 512]
    for tc in range(ntc):
        if pre_tc is not None:
            pre_tc(tc)
        mm_group(ctx, A[0:1, tc * 512: tc * 512 + 512],
                 [(sc[ones_key][:, :], res(k, tc)) for k in range(nch)])
    V = "vector"
    ctx.op(V, lambda e: e.tensor_scalar(mean, A[0:1, 0:tokens], 1.0 / D, None, ALU.mult),
           reads=[A[0:1, 0:tokens]], writes=[mean])
    for tc in range(ntc):
        if pre_tc is not None:
            pre_tc(ntc + tc)
        for k in range(nch):
            slot = ((tc * nch + k) % sq_slots) * 512
            ctx.op("scalar", lambda e, k=k, tc=tc, slot=slot: e.activation(
                sqscratch[:, slot:slot + 512], res(k, tc), AF.Square),
                reads=[res(k, tc)], writes=[sqscratch[:, slot:slot + 512]])
            lh = sc["ones_colb"][:, :]
            rh = sqscratch[:, slot: slot + 512]
            pt = A[0:1, tc * 512: tc * 512 + 512]
            if k == 0:
                ctx.op("tensor", lambda e, pt=pt, lh=lh, rh=rh, st=(nch == 1): e.matmul(
                    pt, lh, rh, start=True, stop=st),
                    reads=[lh, rh], war_writes=[pt], provide=False)
            elif k == nch - 1:
                opid = ctx.op("tensor", lambda e, pt=pt, lh=lh, rh=rh: e.matmul(
                    pt, lh, rh, start=False, stop=True), reads=[lh, rh])
                ctx.record_write(pt, opid)
            else:
                ctx.op("tensor", lambda e, pt=pt, lh=lh, rh=rh: e.matmul(
                    pt, lh, rh, start=False, stop=False), reads=[lh, rh])
    ctx.op(V, lambda e: e.tensor_tensor(tmp, mean, mean, ALU.mult),
           reads=[mean], writes=[tmp])
    ctx.op(V, lambda e: e.tensor_scalar(a_row, A[0:1, 0:tokens], 1.0 / D, None, ALU.mult),
           reads=[A[0:1, 0:tokens]], writes=[a_row])
    ctx.op(V, lambda e: e.tensor_tensor(a_row, a_row, tmp, ALU.subtract),
           reads=[a_row, tmp], writes=[a_row])
    ctx.op(V, lambda e: e.tensor_scalar(a_row, a_row, EPS, None, ALU.add),
           reads=[a_row], writes=[a_row])
    ctx.op("scalar", lambda e: e.activation(tmp, a_row, AF.Sqrt),
           reads=[a_row], writes=[tmp])
    ctx.op(V, lambda e: e.reciprocal(a_row, tmp), reads=[tmp], writes=[a_row])
    ctx.op(V, lambda e: e.tensor_scalar(tmp, mean, -1.0, None, ALU.mult),
           reads=[mean], writes=[tmp])
    ctx.op(V, lambda e: e.tensor_tensor(b_row, tmp, a_row, ALU.mult),
           reads=[tmp, a_row], writes=[b_row])


def emit_ln_apply(ctx, res, xn_slices, sc, tcs, nch=NCH, tokens=T, pre_tc=None):
    A = sc["arenaA"]
    rows = sc["rows"]
    RT = sc["rows_tokens"]
    a_row = rows[:, 2 * RT:2 * RT + tokens]
    b_row = rows[:, 0 * RT:0 * RT + tokens]
    V = "vector"
    for i, tc in enumerate(tcs):
        if pre_tc is not None:
            pre_tc(tc)
        ta = A[:, (2 * i % 4) * 512:(2 * i % 4) * 512 + 512]
        tb = A[:, (2 * i % 4 + 1) * 512:(2 * i % 4 + 1) * 512 + 512]
        mm_group(ctx, ta, [(sc["ones_row"][0:1, 0:128], a_row[0:1, tc * 512: tc * 512 + 512])])
        mm_group(ctx, tb, [(sc["ones_row"][0:1, 0:128], b_row[0:1, tc * 512: tc * 512 + 512])])
        for k in range(nch):
            if callable(res):
                rs = res(k, tc)
            else:
                rs = res[:, k * tokens + tc * 512: k * tokens + tc * 512 + 512]
            dst = xn_slices(k, i)
            ctx.op(V, lambda e, rs=rs, ta=ta: e.tensor_tensor(
                sc["apt"][:, :], rs, ta, ALU.mult),
                reads=[rs, ta], writes=[sc["apt"][:, :]])
            ctx.op(V, lambda e, dst=dst, tb=tb: e.tensor_tensor(
                dst, sc["apt"][:, :], tb, ALU.add),
                reads=[sc["apt"][:, :], tb], writes=[dst])


def emit_ln(ctx, res, xn, sc, nch=NCH, tokens=T, ones_key="ones_col"):
    emit_ln_stats(ctx, res, sc, xn, nch=nch, tokens=tokens, ones_key=ones_key,
                  sq_slots=nch * (tokens // 512))
    emit_ln_apply(ctx, res,
                  (lambda k, i: xn[:, k * tokens + i * 512: k * tokens + i * 512 + 512]),
                  sc, list(range(tokens // 512)), nch=nch, tokens=tokens)


def build_l1_ffn(nc, es):
    ctx = Ctx(nc, es)
    x_d = nc.dram_tensor("x_fm", [D, T], F32, kind="ExternalInput").ap()
    w_d = nc.dram_tensor("w_t", [D, D], F32, kind="ExternalInput").ap()
    y_d = nc.dram_tensor("y_fm", [D, T], F32, kind="ExternalOutput").ap()

    res = ctx.sbuf("res", [128, NCH * T], F32)
    xn = ctx.sbuf("xn", [128, NCH * T], BF16)
    wf = ctx.sbuf("wf", [128, NCH * D], F32)
    wq = ctx.sbuf("wq", [128, NCH * D], BF16)
    tern = ctx.sbuf("tern", [128, 2 * 1024], F32)
    trows = ctx.sbuf("trows", [128, 16], F32)
    sc = make_common(ctx)

    ctx.dma("sync", res.rearrange("p (c t) -> p c t", c=NCH),
            x_d.rearrange("(c p) t -> p c t", p=128))
    ctx.dma("gpsimd", wf.rearrange("p (c o) -> p c o", c=NCH),
            w_d.rearrange("(c p) o -> p c o", p=128))

    A = sc["arenaA"]
    m1 = tern[:, 0:1024]
    m2 = tern[:, 1024:2048]
    for k in range(NCH):
        ctx.op("vector", lambda e, k=k: e.reduce_sum(
            trows[:, k:k + 1], wf[:, k * D:(k + 1) * D], axis=AX.X,
            apply_absolute_value=True),
            reads=[wf[:, k * D:(k + 1) * D]], writes=[trows[:, k:k + 1]])
    ctx.op("vector", lambda e: e.reduce_sum(trows[:, 8:9], trows[:, 0:8], axis=AX.X),
           reads=[trows[:, 0:8]], writes=[trows[:, 8:9]])
    mm_group(ctx, A[0:1, 1536:1537], [(trows[:, 8:9], sc["ones_col"][:, :])])
    ctx.op("vector", lambda e: e.tensor_scalar(trows[0:1, 9:10], A[0:1, 1536:1537],
                                               1.0 / (D * D), None, ALU.mult),
           reads=[A[0:1, 1536:1537]], writes=[trows[0:1, 9:10]])
    mm_group(ctx, A[:, 1024:1025], [(sc["ones_row"][0:1, 0:128], trows[0:1, 9:10])])
    ctx.op("vector", lambda e: e.tensor_copy(trows[:, 10:11], A[:, 1024:1025]),
           reads=[A[:, 1024:1025]], writes=[trows[:, 10:11]])
    ctx.op("vector", lambda e: e.tensor_scalar(trows[:, 11:12], trows[:, 10:11],
                                               0.5, None, ALU.mult),
           reads=[trows[:, 10:11]], writes=[trows[:, 11:12]])
    ctx.op("vector", lambda e: e.tensor_scalar(trows[:, 12:13], trows[:, 11:12],
                                               -1.0, None, ALU.mult),
           reads=[trows[:, 11:12]], writes=[trows[:, 12:13]])
    for k in range(NCH):
        wk = wf[:, k * D:(k + 1) * D]
        ctx.op("vector", lambda e, wk=wk: e.tensor_scalar(m1, wk, trows[:, 11:12], None, ALU.is_ge),
               reads=[wk, trows[:, 11:12]], writes=[m1])
        ctx.op("vector", lambda e, wk=wk: e.tensor_scalar(m2, wk, trows[:, 12:13], None, ALU.is_le),
               reads=[wk, trows[:, 12:13]], writes=[m2])
        ctx.op("vector", lambda e, k=k: e.tensor_tensor(
            wq[:, k * D:(k + 1) * D], m1, m2, ALU.subtract),
            reads=[m1, m2], writes=[wq[:, k * D:(k + 1) * D]])

    emit_ln(ctx, res, xn, sc)

    Bb = sc["arenaB"]
    for j in range(NCH):
        for tc in range(NTC):
            pt = Bb[:, (j * NTC + tc) % 4 * 512: ((j * NTC + tc) % 4 + 1) * 512]
            mm_group(ctx, pt,
                     [(wq[:, k * D + j * 128: k * D + j * 128 + 128],
                       xn[:, k * T + tc * 512: k * T + tc * 512 + 512])
                      for k in range(NCH)])
            off = j * T + tc * 512
            ctx.op("vector", lambda e, off=off, pt=pt: e.scalar_tensor_tensor(
                res[:, off: off + 512], pt, trows[:, 10:11], res[:, off: off + 512],
                ALU.mult, ALU.add),
                reads=[pt, trows[:, 10:11], res[:, off: off + 512]],
                writes=[res[:, off: off + 512]])

    st = ctx.dma("sync", y_d.rearrange("(c p) t -> p c t", p=128),
                 res.rearrange("p (c t) -> p c t", c=NCH))
    ctx.barrier_output(st)
    ctx.finalize()
    return nc


def _fm_shard(x):
    out = []
    for c in range(N_CORES):
        xc = x[:, c * S_LOC:(c + 1) * S_LOC, :]
        out.append(np.ascontiguousarray(xc.transpose(2, 0, 1).reshape(D, T)))
    return out


def _fm_unshard(shards):
    x = np.empty((B, S, D), dtype=shards[0].dtype)
    for c in range(N_CORES):
        x[:, c * S_LOC:(c + 1) * S_LOC, :] = \
            shards[c].reshape(D, B, S_LOC).transpose(1, 2, 0)
    return x


def build_l0_ffn(nc, es):
    ctx = Ctx(nc, es)
    x_d = nc.dram_tensor("x_fm", [D, T], F32, kind="ExternalInput").ap()
    H = 4096
    NH = H // 128
    w1t_d = nc.dram_tensor("w1t", [D, H], F32, kind="ExternalInput").ap()
    w2t_d = nc.dram_tensor("w2t", [D, H], F32, kind="ExternalInput").ap()
    h1t_d = nc.dram_tensor("h1t", [H, H], F32, kind="ExternalInput").ap()
    h2t_d = nc.dram_tensor("h2t", [H, H], F32, kind="ExternalInput").ap()
    w3t_d = nc.dram_tensor("w3t", [H, D], F32, kind="ExternalInput").ap()
    s1_d = nc.dram_tensor("w1s", [128, NH], F32, kind="ExternalInput").ap()
    s2_d = nc.dram_tensor("w2s", [128, NH], F32, kind="ExternalInput").ap()
    sh1_d = nc.dram_tensor("h1s", [128, NH], F32, kind="ExternalInput").ap()
    sh2_d = nc.dram_tensor("h2s", [128, NH], F32, kind="ExternalInput").ap()
    s3_d = nc.dram_tensor("w3s", [128, NCH], F32, kind="ExternalInput").ap()
    y_d = nc.dram_tensor("y_fm", [D, T], F32, kind="ExternalOutput").ap()

    TQ = 512
    res = ctx.sbuf("res", [128, NCH * T], F32)
    xnq = ctx.sbuf("xnq", [128, NCH * TQ], BF16)
    hbuf = ctx.sbuf("hbuf", [128, NH * TQ], BF16)
    gbuf = ctx.sbuf("gbuf", [128, NH * TQ], BF16)
    scl = ctx.sbuf("scl", [128, 4 * NH + NCH], F32)
    wb1 = ctx.sbuf("wb1", [128, NCH * 128], BF16)
    wb1b = ctx.sbuf("wb1b", [128, NCH * 128], BF16)
    wb2 = ctx.sbuf("wb2", [128, NH * 128], BF16)
    wb2b = ctx.sbuf("wb2b", [128, NH * 128], BF16)
    sc = make_common(ctx)

    ctx.dma("sync", res.rearrange("p (c t) -> p c t", c=NCH),
            x_d.rearrange("(c p) t -> p c t", p=128))
    ctx.dma("sync", scl[:, 0:NH], s1_d)
    ctx.dma("sync", scl[:, NH:2 * NH], s2_d)
    ctx.dma("sync", scl[:, 2 * NH:3 * NH], sh1_d)
    ctx.dma("sync", scl[:, 3 * NH:4 * NH], sh2_d)
    ctx.dma("sync", scl[:, 4 * NH:4 * NH + NCH], s3_d)

    emit_ln_stats(ctx, res, sc, hbuf)

    Bb = sc["arenaB"]
    nb = [0]

    def bank():
        pt = Bb[:, (nb[0] % 4) * 512: (nb[0] % 4) * 512 + 512]
        nb[0] += 1
        return pt

    def wslice(dram, j, buf):
        src = dram.rearrange("(k p) o -> p k o", p=128)[:, :, j * 128:(j + 1) * 128]
        dst = buf.rearrange("p (k o) -> p k o", o=128)
        ctx.dma("gpsimd", dst, src)
        return buf

    for tq in range(4):
        tb = tq * 512
        emit_ln_apply(ctx, res,
                      (lambda k, i: xnq[:, k * TQ: k * TQ + 512]),
                      sc, [tq])
        for j in range(NH):
            b1 = wslice(w1t_d, j, wb1 if j % 2 == 0 else wb1b)
            pt = bank()
            mm_group(ctx, pt, [(b1[:, k * 128:(k + 1) * 128],
                                xnq[:, k * TQ: k * TQ + 512]) for k in range(NCH)])
            ctx.op("scalar", lambda e, j=j, pt=pt: e.activation(
                gbuf[:, j * TQ: j * TQ + 512], pt, AF.Silu, scale=scl[:, j:j + 1]),
                reads=[pt, scl[:, j:j + 1]], writes=[gbuf[:, j * TQ: j * TQ + 512]])
        for j in range(NH):
            b1 = wslice(w2t_d, j, wb1 if j % 2 == 0 else wb1b)
            pt = bank()
            mm_group(ctx, pt, [(b1[:, k * 128:(k + 1) * 128],
                                xnq[:, k * TQ: k * TQ + 512]) for k in range(NCH)])
            ctx.op("vector", lambda e, j=j, pt=pt: e.scalar_tensor_tensor(
                hbuf[:, j * TQ: j * TQ + 512], pt, scl[:, NH + j:NH + j + 1],
                gbuf[:, j * TQ: j * TQ + 512], ALU.mult, ALU.mult),
                reads=[pt, scl[:, NH + j:NH + j + 1], gbuf[:, j * TQ: j * TQ + 512]],
                writes=[hbuf[:, j * TQ: j * TQ + 512]])
        for j in range(NH):
            b2 = wslice(h1t_d, j, wb2 if j % 2 == 0 else wb2b)
            pt = bank()
            mm_group(ctx, pt, [(b2[:, k * 128:(k + 1) * 128],
                                hbuf[:, k * TQ: k * TQ + 512]) for k in range(NH)])
            ctx.op("scalar", lambda e, j=j, pt=pt: e.activation(
                gbuf[:, j * TQ: j * TQ + 512], pt, AF.Gelu,
                scale=scl[:, 2 * NH + j:2 * NH + j + 1]),
                reads=[pt, scl[:, 2 * NH + j:2 * NH + j + 1]],
                writes=[gbuf[:, j * TQ: j * TQ + 512]])
        for j in range(NH):
            b2 = wslice(h2t_d, j, wb2 if j % 2 == 0 else wb2b)
            pt = bank()
            mm_group(ctx, pt, [(b2[:, k * 128:(k + 1) * 128],
                                gbuf[:, k * TQ: k * TQ + 512]) for k in range(NH)])
            ctx.op("scalar", lambda e, j=j, pt=pt: e.activation(
                hbuf[:, j * TQ: j * TQ + 512], pt, AF.Gelu,
                scale=scl[:, 3 * NH + j:3 * NH + j + 1]),
                reads=[pt, scl[:, 3 * NH + j:3 * NH + j + 1]],
                writes=[hbuf[:, j * TQ: j * TQ + 512]])
        for j in range(NCH):
            b2 = wslice(w3t_d, j, wb2 if j % 2 == 0 else wb2b)
            pt = bank()
            mm_group(ctx, pt, [(b2[:, k * 128:(k + 1) * 128],
                                hbuf[:, k * TQ: k * TQ + 512]) for k in range(NH)])
            off = j * T + tb
            ctx.op("vector", lambda e, off=off, j=j, pt=pt: e.scalar_tensor_tensor(
                res[:, off: off + 512], pt, scl[:, 4 * NH + j:4 * NH + j + 1],
                res[:, off: off + 512], ALU.mult, ALU.add),
                reads=[pt, scl[:, 4 * NH + j:4 * NH + j + 1], res[:, off: off + 512]],
                writes=[res[:, off: off + 512]])

    st = ctx.dma("sync", y_d.rearrange("(c p) t -> p c t", p=128),
                 res.rearrange("p (c t) -> p c t", c=NCH))
    ctx.barrier_output(st)
    ctx.finalize()
    return nc


def build_l2_ffn(nc, es):
    ctx = Ctx(nc, es)
    x_d = nc.dram_tensor("x_fm", [D, T], F32, kind="ExternalInput").ap()
    xa_d = nc.dram_tensor("xattn_fm", [D, T], BF16, kind="ExternalInput").ap()
    ws_d = nc.dram_tensor("wsum", [128, NCH], F32, kind="ExternalInput").ap()
    HQ = 3072
    NQ = HQ // 128
    wg_d = nc.dram_tensor("wgt", [D, HQ], I32, kind="ExternalInput").ap()
    wu_d = nc.dram_tensor("wut", [D, HQ], I32, kind="ExternalInput").ap()
    wd_d = nc.dram_tensor("wdt", [HQ, D], I32, kind="ExternalInput").ap()
    y_d = nc.dram_tensor("y_fm", [D, T], F32, kind="ExternalOutput").ap()

    TQ = 512
    res = ctx.sbuf("res", [128, NCH * T], F32)
    xnq = ctx.sbuf("xnq", [128, NCH * TQ], BF16)
    gsb = ctx.sbuf("gsb", [128, NQ * TQ], BF16)
    hsb = ctx.sbuf("hsb", [128, NQ * TQ], BF16)
    xai = ctx.sbuf("xai", [128, NCH * TQ], BF16)
    wsum = ctx.sbuf("wsums", [128, NCH], F32)
    wbg = ctx.sbuf("wbg", [128, NCH * 128], BF16)
    wbg2 = ctx.sbuf("wbg2", [128, NCH * 128], BF16)
    wbd = ctx.sbuf("wbd", [128, NQ * 128], BF16)
    wbd2 = ctx.sbuf("wbd2", [128, NQ * 128], BF16)
    sc = make_common(ctx)

    ctx.dma("sync", res.rearrange("p (c t) -> p c t", c=NCH),
            x_d.rearrange("(c p) t -> p c t", p=128))
    ctx.dma("sync", wsum[:, :], ws_d)
    for tq in range(NTC):
        for k in range(NCH):
            ctx.dma("sync", xai[:, k * TQ:(k + 1) * TQ],
                    xa_d.rearrange("(c p) t -> p c t", p=128)[:, k, tq * 512:(tq + 1) * 512])
        for k in range(NCH):
            off = k * T + tq * 512
            ctx.op("vector", lambda e, off=off, k=k: e.scalar_tensor_tensor(
                res[:, off: off + 512], xai[:, k * TQ:(k + 1) * TQ], wsum[:, k:k + 1],
                res[:, off: off + 512], ALU.mult, ALU.add),
                reads=[xai[:, k * TQ:(k + 1) * TQ], wsum[:, k:k + 1], res[:, off: off + 512]],
                writes=[res[:, off: off + 512]])

    emit_ln_stats(ctx, res, sc, gsb, sq_slots=24)

    Bb = sc["arenaB"]
    nb = [0]

    def bank():
        pt = Bb[:, (nb[0] % 4) * 512: (nb[0] % 4) * 512 + 512]
        nb[0] += 1
        return pt

    def wslice(dram, j, buf, nk):
        src = dram.rearrange("(k p) o -> p k o", p=128)[:, :, j * 128:(j + 1) * 128]
        dst = buf.rearrange("p (k o) -> p k o", o=128)
        ctx.dma("gpsimd", dst, src)
        return buf

    SCL = 1.0 / 127.0
    for tq in range(4):
        emit_ln_apply(ctx, res, (lambda k, i: xnq[:, k * TQ: k * TQ + 512]), sc, [tq])
        for j in range(NQ):
            b1 = wslice(wg_d, j, wbg if j % 2 == 0 else wbg2, NCH)
            pt = bank()
            mm_group(ctx, pt, [(b1[:, k * 128:(k + 1) * 128],
                                xnq[:, k * TQ: k * TQ + 512]) for k in range(NCH)])
            ctx.op("scalar", lambda e, j=j, pt=pt: e.activation(
                gsb[:, j * TQ: j * TQ + 512], pt, AF.Silu, scale=SCL),
                reads=[pt], writes=[gsb[:, j * TQ: j * TQ + 512]])
        for j in range(NQ):
            b1 = wslice(wu_d, j, wbg if j % 2 == 0 else wbg2, NCH)
            pt = bank()
            mm_group(ctx, pt, [(b1[:, k * 128:(k + 1) * 128],
                                xnq[:, k * TQ: k * TQ + 512]) for k in range(NCH)])
            ctx.op("vector", lambda e, j=j, pt=pt: e.scalar_tensor_tensor(
                hsb[:, j * TQ: j * TQ + 512], pt, SCL,
                gsb[:, j * TQ: j * TQ + 512], ALU.mult, ALU.mult),
                reads=[pt, gsb[:, j * TQ: j * TQ + 512]],
                writes=[hsb[:, j * TQ: j * TQ + 512]])
        for j in range(NCH):
            b2 = wslice(wd_d, j, wbd if j % 2 == 0 else wbd2, NQ)
            pt = bank()
            mm_group(ctx, pt, [(b2[:, k * 128:(k + 1) * 128],
                                hsb[:, k * TQ: k * TQ + 512]) for k in range(NQ)])
            off = j * T + tq * 512
            ctx.op("vector", lambda e, off=off, pt=pt: e.scalar_tensor_tensor(
                res[:, off: off + 512], pt, SCL,
                res[:, off: off + 512], ALU.mult, ALU.add),
                reads=[pt, res[:, off: off + 512]],
                writes=[res[:, off: off + 512]])

    st = ctx.dma("sync", y_d.rearrange("(c p) t -> p c t", p=128),
                 res.rearrange("p (c t) -> p c t", c=NCH))
    ctx.barrier_output(st)
    ctx.finalize()
    return nc


def build_l3_ffn(nc, es):
    ctx = Ctx(nc, es)
    x_d = nc.dram_tensor("x_fm", [D, T], F32, kind="ExternalInput").ap()
    gw_d = nc.dram_tensor("gate_w", [D, 8], F32, kind="ExternalInput").ap()
    w1_d = nc.dram_tensor("w1", [8, D, 4096], F32, kind="ExternalInput").ap()
    w2_d = nc.dram_tensor("w2", [8, 4096, D], F32, kind="ExternalInput").ap()
    id_d = nc.dram_tensor("ident", [128, 128], F32, kind="ExternalInput").ap()
    y_d = nc.dram_tensor("y_fm", [D, T], F32, kind="ExternalOutput").ap()

    H = 4096
    NH = H // 128
    TQ = 512
    res = ctx.sbuf("res", [128, NCH * T], F32)
    xnq = ctx.sbuf("xnq", [128, NCH * TQ], BF16)
    hbuf = ctx.sbuf("hbuf", [128, NH * TQ], BF16)
    acc = ctx.sbuf("acc", [128, 4 * D], F32)
    comb = ctx.sbuf("comb", [128, 16 * 8], F32)
    glog = ctx.sbuf("glog", [128, 16 * 8], F32)
    gsc = ctx.sbuf("gsc", [128, 16], F32)
    grows = ctx.sbuf("grows", [128, 8], F32)
    gwt = ctx.sbuf("gwt", [128, NCH * 8], BF16)
    ident = ctx.sbuf("identt", [128, 128], F32)
    wb1 = ctx.sbuf("wb1", [128, NCH * 128], BF16)
    wb1b = ctx.sbuf("wb1b", [128, NCH * 128], BF16)
    wb2 = ctx.sbuf("wb2", [128, 1024], BF16)
    wb2b = ctx.sbuf("wb2b", [128, 1024], BF16)
    sc = make_common(ctx)

    ctx.dma("sync", res.rearrange("p (c t) -> p c t", c=NCH),
            x_d.rearrange("(c p) t -> p c t", p=128))
    ctx.dma("sync", ident[:, :], id_d)
    ctx.dma("gpsimd", gwt.rearrange("p (c o) -> p c o", c=NCH),
            gw_d.rearrange("(c p) o -> p c o", p=128))

    emit_ln_stats(ctx, res, sc, hbuf, sq_slots=32)

    A = sc["arenaA"]
    Bb = sc["arenaB"]
    V = "vector"

    for tq in range(4):
        emit_ln_apply(ctx, res, (lambda k, i: xnq[:, k * TQ: k * TQ + 512]), sc, [tq])
        for i in range(4):
            tt = tq * 4 + i
            pt = A[0:128, (i % 2) * 512:(i % 2) * 512 + 8]
            mm_group(ctx, pt, [(xnq[:, k * TQ + i * 128: k * TQ + i * 128 + 128],
                                gwt[:, k * 8:(k + 1) * 8]) for k in range(NCH)])
            gl = glog[:, tt * 8:(tt + 1) * 8]
            cb = comb[:, tt * 8:(tt + 1) * 8]
            l2 = gsc[:, 0:8]
            l3 = gsc[:, 8:16]
            ctx.op(V, lambda e, pt=pt, gl=gl: e.tensor_copy(gl, pt),
                   reads=[pt], writes=[gl])
            ctx.op(V, lambda e, gl=gl: e.reduce_max(grows[:, 0:1], gl, axis=AX.X),
                   reads=[gl], writes=[grows[:, 0:1]])
            ctx.op(V, lambda e: e.tensor_scalar(grows[:, 1:2], grows[:, 0:1], -1.0, None, ALU.mult),
                   reads=[grows[:, 0:1]], writes=[grows[:, 1:2]])
            ctx.op(V, lambda e, gl=gl, cb=cb: e.tensor_scalar(cb, gl, grows[:, 0:1], None, ALU.is_equal),
                   reads=[gl, grows[:, 0:1]], writes=[cb])
            ctx.op(V, lambda e, gl=gl, cb=cb, l2=l2: e.scalar_tensor_tensor(
                l2, cb, -1.0e30, gl, ALU.mult, ALU.add),
                reads=[cb, gl], writes=[l2])
            ctx.op(V, lambda e, l2=l2: e.reduce_max(grows[:, 2:3], l2, axis=AX.X),
                   reads=[l2], writes=[grows[:, 2:3]])
            ctx.op("scalar", lambda e: e.activation(grows[:, 3:4], grows[:, 2:3], AF.Exp,
                                                    bias=grows[:, 1:2], scale=1.0),
                   reads=[grows[:, 2:3], grows[:, 1:2]], writes=[grows[:, 3:4]])
            ctx.op(V, lambda e: e.tensor_scalar(grows[:, 4:5], grows[:, 3:4], 1.0, None, ALU.add),
                   reads=[grows[:, 3:4]], writes=[grows[:, 4:5]])
            ctx.op(V, lambda e: e.reciprocal(grows[:, 5:6], grows[:, 4:5]),
                   reads=[grows[:, 4:5]], writes=[grows[:, 5:6]])
            ctx.op(V, lambda e: e.tensor_tensor(grows[:, 6:7], grows[:, 3:4], grows[:, 5:6], ALU.mult),
                   reads=[grows[:, 3:4], grows[:, 5:6]], writes=[grows[:, 6:7]])
            ctx.op(V, lambda e, l2=l2, l3=l3: e.tensor_scalar(l3, l2, grows[:, 2:3], None, ALU.is_equal),
                   reads=[l2, grows[:, 2:3]], writes=[l3])
            ctx.op(V, lambda e, cb=cb: e.tensor_scalar(cb, cb, grows[:, 5:6], None, ALU.mult),
                   reads=[cb, grows[:, 5:6]], writes=[cb])
            ctx.op(V, lambda e, cb=cb, l3=l3: e.scalar_tensor_tensor(
                cb, l3, grows[:, 6:7], cb, ALU.mult, ALU.add),
                reads=[l3, grows[:, 6:7], cb], writes=[cb])

        for e_i in range(8):
            for j in range(NH):
                buf = wb1 if j % 2 == 0 else wb1b
                src = w1_d[e_i].rearrange("(k p) o -> p k o", p=128)[:, :, j * 128:(j + 1) * 128]
                ctx.dma("gpsimd", buf.rearrange("p (k o) -> p k o", o=128), src)
                pt = Bb[:, (j % 4) * 512: (j % 4) * 512 + 512]
                mm_group(ctx, pt, [(buf[:, k * 128:(k + 1) * 128],
                                    xnq[:, k * TQ: k * TQ + 512]) for k in range(NCH)])
                ctx.op("scalar", lambda e, j=j, pt=pt: e.activation(
                    hbuf[:, j * TQ: j * TQ + 512], pt, AF.Gelu),
                    reads=[pt], writes=[hbuf[:, j * TQ: j * TQ + 512]])
            ends = {}
            for j in range(NH):
                buf2 = wb2 if j % 2 == 0 else wb2b
                src2 = w2_d[e_i].rearrange("(k p) o -> p k o", p=128)[:, j, :]
                ctx.dma("gpsimd", buf2[:, :], src2)
                for tt in range(4):
                    for oc in range(2):
                        bank_i = tt * 2 + oc
                        arena = A if bank_i < 4 else Bb
                        pt = arena[:, (bank_i % 4) * 512: (bank_i % 4) * 512 + 512]
                        lh = hbuf[:, j * TQ + tt * 128: j * TQ + tt * 128 + 128]
                        rh = buf2[:, oc * 512:(oc + 1) * 512]
                        if j == 0:
                            ctx.op("tensor", lambda e, pt=pt, lh=lh, rh=rh: e.matmul(
                                pt, lh, rh, start=True, stop=False),
                                reads=[lh, rh], war_writes=[pt], provide=False)
                        elif j == NH - 1:
                            op = ctx.op("tensor", lambda e, pt=pt, lh=lh, rh=rh: e.matmul(
                                pt, lh, rh, start=False, stop=True),
                                reads=[lh, rh])
                            ctx.record_write(pt, op)
                        else:
                            ctx.op("tensor", lambda e, pt=pt, lh=lh, rh=rh: e.matmul(
                                pt, lh, rh, start=False, stop=False),
                                reads=[lh, rh])
            for tt in range(4):
                gtt = tq * 4 + tt
                ccol = comb[:, gtt * 8 + e_i: gtt * 8 + e_i + 1]
                for oc in range(2):
                    bank_i = tt * 2 + oc
                    arena = A if bank_i < 4 else Bb
                    pt = arena[:, (bank_i % 4) * 512: (bank_i % 4) * 512 + 512]
                    dst = acc[:, tt * D + oc * 512: tt * D + oc * 512 + 512]
                    if e_i == 0:
                        ctx.op(V, lambda e, pt=pt, dst=dst, ccol=ccol: e.tensor_scalar(
                            dst, pt, ccol, None, ALU.mult),
                            reads=[pt, ccol], writes=[dst])
                    else:
                        ctx.op(V, lambda e, pt=pt, dst=dst, ccol=ccol: e.scalar_tensor_tensor(
                            dst, pt, ccol, dst, ALU.mult, ALU.add),
                            reads=[pt, ccol, dst], writes=[dst])
        for tt in range(4):
            for och in range(NCH):
                srcv = acc[:, tt * D + och * 128: tt * D + och * 128 + 128]
                bank_i = (tt * NCH + och) % 8
                arena = A if bank_i < 4 else Bb
                pt = arena[:, (bank_i % 4) * 512: (bank_i % 4) * 512 + 128]
                op = ctx.op("tensor", lambda e, pt=pt, srcv=srcv: e.transpose(
                    pt, srcv, ident[:, :]),
                    reads=[srcv, ident[:, :]], war_writes=[pt], provide=False)
                ctx.record_write(pt, op)
                off = och * T + tq * TQ + tt * 128
                ctx.op(V, lambda e, off=off, pt=pt: e.tensor_tensor(
                    res[:, off: off + 128], res[:, off: off + 128], pt, ALU.add),
                    reads=[res[:, off: off + 128], pt],
                    writes=[res[:, off: off + 128]])

    st = ctx.dma("sync", y_d.rearrange("(c p) t -> p c t", p=128),
                 res.rearrange("p (c t) -> p c t", c=NCH))
    ctx.barrier_output(st)
    ctx.finalize()
    return nc


def build_l3_attn(nc, es, phase=99):
    ctx = Ctx(nc, es)
    x_d = nc.dram_tensor("x_fm", [D, T], F32, kind="ExternalInput").ap()
    xe_d = nc.dram_tensor("x_ext", [D, 4096], F32, kind="ExternalInput").ap()
    val_d = nc.dram_tensor("valid", [1, 4096], BF16, kind="ExternalInput").ap()
    wq_d = nc.dram_tensor("wq", [D, D], F32, kind="ExternalInput").ap()
    wk_d = nc.dram_tensor("wk", [D, D], F32, kind="ExternalInput").ap()
    wv_d = nc.dram_tensor("wv", [D, D], F32, kind="ExternalInput").ap()
    wo_d = nc.dram_tensor("wo_sl", [64, 16 * D], F32, kind="ExternalInput").ap()
    bq_d = nc.dram_tensor("bq_c", [128, NCH], F32, kind="ExternalInput").ap()
    bkr_d = nc.dram_tensor("bk_r", [1, D], BF16, kind="ExternalInput").ap()
    bvr_d = nc.dram_tensor("bv_r", [1, D], BF16, kind="ExternalInput").ap()
    bor_d = nc.dram_tensor("bo_r", [1, D], BF16, kind="ExternalInput").ap()
    mk_d = nc.dram_tensor("maskT", [3, 128, 512], BF16, kind="ExternalInput").ap()
    y_d = nc.dram_tensor("y_fm", [D, T], F32, kind="ExternalOutput").ap()

    TE_ = 1024
    res = ctx.sbuf("res", [128, NCH * T], F32)
    xb = ctx.sbuf("xb", [128, NCH * TE_], BF16)
    xnb = ctx.sbuf("xnb", [128, NCH * TE_], BF16)
    qb = ctx.sbuf("qb", [128, NCH * 512], BF16)
    kb = ctx.sbuf("kb", [128, NCH * 768], BF16)
    vb = ctx.sbuf("vb", [128, 6 * 1040], BF16)
    pb = ctx.sbuf("pb", [128, 4 * 512], BF16)
    ao64 = ctx.sbuf("ao64", [64, 16 * 512], BF16)
    wob = ctx.sbuf("wob", [64, 16 * 128], BF16)
    wob2 = ctx.sbuf("wob2", [64, 16 * 128], BF16)
    wbuf = ctx.sbuf("wbuf", [128, NCH * 512], BF16)
    wbuf2 = ctx.sbuf("wbuf2", [128, NCH * 128], BF16)
    biases = ctx.sbuf("biases", [128, 2 * NCH], F32)
    brow = ctx.sbuf("brow", [1, 3 * D], BF16)
    valids = ctx.sbuf("valids", [1, 1024], BF16)
    maskt = ctx.sbuf("maskt", [128, 3 * 512], BF16)
    bcs = ctx.sbuf("bcs", [64, 2048], BF16)
    sc = make_common(ctx, rows_tokens=1024)
    recr = sc["rows"][:, 1024:3072]

    ctx.dma("sync", res.rearrange("p (c t) -> p c t", c=NCH),
            x_d.rearrange("(c p) t -> p c t", p=128))
    ctx.dma("sync", biases[:, 0:NCH], bq_d)
    ctx.dma("sync", brow[0:1, 0:D], bkr_d)
    ctx.dma("sync", brow[0:1, D:2 * D], bvr_d)
    ctx.dma("sync", brow[0:1, 2 * D:3 * D], bor_d)
    ctx.dma("sync", maskt.rearrange("p (i q) -> p i q", i=3),
            mk_d.rearrange("i p q -> p i q"))
    ctx.op("vector", lambda e: e.tensor_scalar(biases[:, NCH:2 * NCH], biases[:, 0:NCH],
                                               0.125, None, ALU.mult),
           reads=[biases[:, 0:NCH]], writes=[biases[:, NCH:2 * NCH]])

    A = sc["arenaA"]
    Bb = sc["arenaB"]
    V = "vector"
    nb = [0]

    def bank(width=512):
        pt = Bb[:, (nb[0] % 4) * 512: (nb[0] % 4) * 512 + width]
        nb[0] += 1
        return pt

    for b in range(B):
        if phase < 1:
            break
        ctx.dma("sync", valids[:, :], val_d[0:1, b * TE_:(b + 1) * TE_])
        ctx.dma("gpsimd", xb.rearrange("p (c t) -> p c t", c=NCH),
                xe_d.rearrange("(c p) t -> p c t", p=128)[:, :, b * TE_:(b + 1) * TE_])
        emit_ln(ctx, xb, xnb, sc, tokens=TE_, ones_key="ones_colb")
        if phase < 2:
            continue
        for j in range(NCH):
            ctx.dma("gpsimd", wbuf2.rearrange("p (k o) -> p k o", o=128),
                    wq_d.rearrange("(k p) o -> p k o", p=128)[:, :, j * 128:(j + 1) * 128])
            pt = bank()
            mm_group(ctx, pt, [(wbuf2[:, k * 128:(k + 1) * 128],
                                xnb[:, k * TE_ + 128: k * TE_ + 640]) for k in range(NCH)])
            ctx.op("scalar", lambda e, j=j, pt=pt: e.activation(
                qb[:, j * 512:(j + 1) * 512], pt, AF.Identity,
                bias=biases[:, NCH + j:NCH + j + 1], scale=0.125),
                reads=[pt, biases[:, NCH + j:NCH + j + 1]],
                writes=[qb[:, j * 512:(j + 1) * 512]])
        for j in range(NCH):
            ctx.dma("gpsimd", wbuf2.rearrange("p (k o) -> p k o", o=128),
                    wk_d.rearrange("(k p) o -> p k o", p=128)[:, :, j * 128:(j + 1) * 128])
            for (c0, cw) in [(0, 512), (512, 256)]:
                pt = bank(cw)
                parts = [(wbuf2[:, k * 128:(k + 1) * 128],
                          xnb[:, k * TE_ + c0: k * TE_ + c0 + cw]) for k in range(NCH)]
                parts.append((brow[0:1, j * 128:(j + 1) * 128],
                              valids[0:1, c0: c0 + cw]))
                mm_group(ctx, pt, parts)
                ctx.op("scalar", lambda e, j=j, pt=pt, c0=c0, cw=cw: e.activation(
                    kb[:, j * 768 + c0: j * 768 + c0 + cw], pt, AF.Identity),
                    reads=[pt], writes=[kb[:, j * 768 + c0: j * 768 + c0 + cw]])
        if phase < 3:
            continue
        ctx.op("gpsimd", lambda e: e.memset(
            vb.rearrange("p (i h o) -> p i h o", i=6, h=16)[:, :, :, 64:65], 1.0),
            writes=[vb[:, :]])
        for oc in range(2):
            ctx.dma("gpsimd", wbuf.rearrange("p (k o) -> p k o", o=512),
                    wv_d.rearrange("(k p) o -> p k o", p=128)[:, :, oc * 512:(oc + 1) * 512])
            for kt in range(6):
                pt = bank()
                parts = [(xnb[:, k * TE_ + kt * 128: k * TE_ + kt * 128 + 128],
                          wbuf[:, k * 512:(k + 1) * 512]) for k in range(NCH)]
                parts.append((valids[0:1, kt * 128: kt * 128 + 128],
                              brow[0:1, D + oc * 512: D + oc * 512 + 512]))
                mm_group(ctx, pt, parts)
                dsts = vb.rearrange("p (i h o) -> p i h o", i=6, h=16)[:, kt, oc * 8:(oc + 1) * 8, 0:64]
                ctx.op(V, lambda e, pt=pt, dsts=dsts: e.tensor_copy(
                    dsts, pt.rearrange("p (h o) -> p h o", h=8)),
                    reads=[pt], writes=[dsts])
        if phase < 4:
            continue
        for n in range(4):
            for hg in range(4):
                for kt in range(3):
                    pslot = pb[:, ((hg * 3 + kt) % 4) * 512: ((hg * 3 + kt) % 4) * 512 + 512]
                    for hi in range(4):
                        h = hg * 4 + hi
                        lh = kb[(h % 2) * 64:(h % 2) * 64 + 64,
                                (h // 2) * 768 + (n + kt) * 128: (h // 2) * 768 + (n + kt) * 128 + 128]
                        rh = qb[(h % 2) * 64:(h % 2) * 64 + 64,
                                (h // 2) * 512 + n * 128: (h // 2) * 512 + n * 128 + 128]
                        pt = Bb[:, hi * 512: hi * 512 + 128]
                        opid = ctx.op("tensor", lambda e, pt=pt, lh=lh, rh=rh: e.matmul(
                            pt, lh, rh, start=True, stop=True),
                            reads=[lh, rh], war_writes=[pt], provide=False)
                        ctx.record_write(pt, opid)
                        ctx.op("scalar", lambda e, pt=pt, pslot=pslot, hi=hi: e.activation(
                            pslot[:, hi * 128:(hi + 1) * 128], pt, AF.Exp),
                            reads=[pt], writes=[pslot[:, hi * 128:(hi + 1) * 128]])
                    if phase >= 42 or phase == 99:
                        ctx.op(V, lambda e, pslot=pslot, kt=kt: e.tensor_tensor(
                            pslot, pslot, maskt[:, kt * 512:(kt + 1) * 512], ALU.mult),
                            reads=[pslot, maskt[:, kt * 512:(kt + 1) * 512]], writes=[pslot])
                if 43 > phase and phase != 99:
                    continue
                for hi in range(4):
                    h = hg * 4 + hi
                    pav = A[0:65, hi * 512: hi * 512 + 128]
                    parts = []
                    for kt in range(3):
                        lh = vb.rearrange("p (i c) -> p i c", i=6)[:, n + kt, h * 65: h * 65 + 65]
                        rh = pb[:, ((hg * 3 + kt) % 4) * 512 + hi * 128:
                                ((hg * 3 + kt) % 4) * 512 + hi * 128 + 128]
                        parts.append((lh, rh))
                    mm_group(ctx, pav, parts)
                if phase < 5 and phase != 99:
                    continue
                dvs = bass.AP(A, 64 * 2048 + 0, [[2048, 1], [512, 4], [1, 128]])
                ctx.op(V, lambda e, dvs=dvs: e.reciprocal(recr[0:1, 0:512], dvs),
                       reads=[A[64:65, 0:2048]], writes=[recr[0:1, 0:512]])
                for q4 in range(1):
                    pb2 = Bb[0:64, 0:512]
                    mm_group(ctx, pb2, [(sc["ones_row"][0:1, 0:64], recr[0:1, 0:512])])
                    ctx.op(V, lambda e, pb2=pb2: e.tensor_copy(bcs[:, 0:512], pb2),
                           reads=[pb2], writes=[bcs[:, 0:512]])
                in0 = bass.AP(A, 0, [[2048, 64], [512, 4], [1, 128]])
                in1 = bass.AP(bcs, 0, [[2048, 64], [128, 4], [1, 128]])
                dst = bass.AP(ao64, (hg * 4) * 512 + n * 128,
                              [[16 * 512, 64], [512, 4], [1, 128]])
                ctx.op(V, lambda e, in0=in0, in1=in1, dst=dst: e.tensor_tensor(
                    dst, in0, in1, ALU.mult),
                    reads=[A[0:64, 0:2048], bcs[:, 0:512]], writes=[dst])
        if phase < 6:
            continue
        for j in range(NCH):
            wb_ = wob if j % 2 == 0 else wob2
            ctx.dma("gpsimd", wb_.rearrange("p (h o) -> p h o", h=16),
                    wo_d.rearrange("p (h o) -> p h o", h=16)[:, :, j * 128:(j + 1) * 128])
            pt = bank()
            parts = [(wb_[:, h * 128:(h + 1) * 128],
                      ao64[:, h * 512:(h + 1) * 512]) for h in range(16)]
            parts.append((brow[0:1, 2 * D + j * 128: 2 * D + j * 128 + 128],
                          sc["ones_rowb"][0:1, 0:512]))
            mm_group(ctx, pt, parts)
            off = j * T + b * 512
            ctx.op(V, lambda e, off=off, pt=pt: e.tensor_tensor(
                res[:, off: off + 512], res[:, off: off + 512], pt, ALU.add),
                reads=[res[:, off: off + 512], pt], writes=[res[:, off: off + 512]])

    st = ctx.dma("sync", y_d.rearrange("(c p) t -> p c t", p=128),
                 res.rearrange("p (c t) -> p c t", c=NCH))
    ctx.barrier_output(st)
    ctx.finalize()
    return nc


def build_l1_attn(nc, es):
    ctx = Ctx(nc, es)
    x_d = nc.dram_tensor("x_fm", [D, T], F32, kind="ExternalInput").ap()
    wqt_d = nc.dram_tensor("wqt", [D, 512], F32, kind="ExternalInput").ap()
    bq_d = nc.dram_tensor("bq_c", [128, 4], F32, kind="ExternalInput").ap()
    wp_d = nc.dram_tensor("wp", [1, 12], F32, kind="ExternalInput").ap()
    pm_d = nc.dram_tensor("permT", [512, 512], BF16, kind="ExternalInput").ap()
    e4_d = nc.dram_tensor("e4b", [128, 4], BF16, kind="ExternalInput").ap()
    sel_d = nc.dram_tensor("selc", [4, 4 * 128], BF16, kind="ExternalInput").ap()
    wot_d = nc.dram_tensor("wot", [D, D], F32, kind="ExternalInput").ap()
    bo_d = nc.dram_tensor("bo_r", [1, D], BF16, kind="ExternalInput").ap()
    y_d = nc.dram_tensor("y_fm", [D, T], F32, kind="ExternalOutput").ap()

    xq = ctx.sbuf("xq", [128, 2 * NCH * 512], F32)
    xn = ctx.sbuf("xn", [128, NCH * T], BF16)
    q = ctx.sbuf("q", [128, 4 * T], BF16)
    ktmp = ctx.sbuf("ktmp", [128, 4 * 512], BF16)
    k2 = ctx.sbuf("k2", [128, 12 * 512], BF16)
    wpb = ctx.sbuf("wpb", [128, 12], F32)
    wpr = ctx.sbuf("wpr", [1, 12], F32)
    acp = ctx.sbuf("acp", [4, 3 * T], BF16)
    bc = ctx.sbuf("bc", [128, 12 * 512], BF16)
    vv = ctx.sbuf("vv", [128, 24 * 512], BF16)
    ol = ctx.sbuf("ol", [128, NCH * 512], BF16)
    ybuf = ctx.sbuf("ybuf", [128, 2 * 512], F32)
    prodb = ctx.sbuf("prodb", [128, 2 * 512], BF16)
    e4b = ctx.sbuf("e4bs", [128, 4], BF16)
    selc = ctx.sbuf("selcs", [4, 4 * 128], BF16)
    bqc = ctx.sbuf("bqc", [128, 4], F32)
    bor = ctx.sbuf("bor", [1, D], BF16)
    wbuf = ctx.sbuf("wbuf", [128, NCH * 128], BF16)
    wbuf2 = ctx.sbuf("wbuf2", [128, NCH * 128], BF16)
    pmb = ctx.sbuf("pmb", [128, 4 * 512], BF16)
    sc = make_common(ctx)

    ctx.dma("sync", bqc[:, :], bq_d)
    ctx.dma("sync", wpr[:, :], wp_d)
    ctx.dma("sync", e4b[:, :], e4_d)
    ctx.dma("sync", selc[:, :], sel_d)
    ctx.dma("sync", bor[:, :], bo_d)
    ctx.dma("sync", pmb.rearrange("p (c o) -> p c o", c=4),
            pm_d.rearrange("(c p) o -> p c o", p=128))
    A = sc["arenaA"]
    Bb = sc["arenaB"]
    V = "vector"

    def ldx(tc):
        tce = tc % NTC
        ctx.dma("sync", xq.rearrange("p (r c t) -> p r c t", r=2, c=NCH)[:, tce % 2],
                x_d.rearrange("(c p) t -> p c t", p=128)[:, :, tce * 512:(tce + 1) * 512])

    def xqs(k, tc):
        return xq[:, (tc % 2) * NCH * 512 + k * 512: (tc % 2) * NCH * 512 + k * 512 + 512]

    emit_ln_stats(ctx, xqs, sc, xn, sq_slots=NCH * NTC, pre_tc=ldx)
    emit_ln_apply(ctx, xqs,
                  (lambda k, i: xn[:, k * T + i * 512: k * T + i * 512 + 512]),
                  sc, list(range(NTC)), pre_tc=ldx)

    for j in range(4):
        ctx.dma("gpsimd", (wbuf if j % 2 == 0 else wbuf2).rearrange("p (k o) -> p k o", o=128),
                wqt_d.rearrange("(k p) o -> p k o", p=128)[:, :, j * 128:(j + 1) * 128])
        wb_ = wbuf if j % 2 == 0 else wbuf2
        for tc in range(NTC):
            pt = Bb[:, (tc % 4) * 512:(tc % 4) * 512 + 512]
            mm_group(ctx, pt, [(wb_[:, k * 128:(k + 1) * 128],
                               xn[:, k * T + tc * 512: k * T + tc * 512 + 512])
                              for k in range(NCH)])
            ctx.op("scalar", lambda e, j=j, tc=tc, pt=pt: e.activation(
                q[:, j * T + tc * 512: j * T + tc * 512 + 512], pt, AF.Identity,
                bias=bqc[:, j:j + 1], scale=1.0),
                reads=[pt, bqc[:, j:j + 1]],
                writes=[q[:, j * T + tc * 512: j * T + tc * 512 + 512]])

    mm_group(ctx, A[:, 0:12], [(sc["ones_row"][0:1, 0:128], wpr[0:1, :])])
    ctx.op(V, lambda e: e.tensor_copy(wpb[:, :], A[:, 0:12]),
           reads=[A[:, 0:12]], writes=[wpb[:, :]])

    for p in range(3):
        for c in range(4):
            kt = ktmp[:, c * 512:(c + 1) * 512]
            ctx.op(V, lambda e, p=p, c=c, kt=kt: e.tensor_scalar(
                kt, q[:, c * T + 0 * 512: c * T + 512], wpb[:, 0 * 3 + p: 0 * 3 + p + 1],
                None, ALU.mult),
                reads=[q[:, c * T: c * T + 512], wpb[:, p:p + 1]], writes=[kt])
            for l in range(1, 4):
                ctx.op(V, lambda e, p=p, c=c, l=l, kt=kt: e.scalar_tensor_tensor(
                    kt, q[:, c * T + l * 512: c * T + l * 512 + 512],
                    wpb[:, l * 3 + p: l * 3 + p + 1], kt, ALU.mult, ALU.add),
                    reads=[q[:, c * T + l * 512: c * T + l * 512 + 512],
                           wpb[:, l * 3 + p: l * 3 + p + 1], kt],
                    writes=[kt])
        for c2 in range(4):
            pt = Bb[:, (c2 % 4) * 512:(c2 % 4) * 512 + 512]
            mm_group(ctx, pt, [(pmb[:, cs * 512 + c2 * 128: cs * 512 + c2 * 128 + 128],
                                ktmp[:, cs * 512:(cs + 1) * 512]) for cs in range(4)])
            ctx.op(V, lambda e, p=p, c2=c2, pt=pt: e.tensor_copy(
                k2[:, (p * 4 + c2) * 512: (p * 4 + c2) * 512 + 512], pt),
                reads=[pt], writes=[k2[:, (p * 4 + c2) * 512: (p * 4 + c2) * 512 + 512]])

    ctx.op(V, lambda e: e.memset(bc[:, :], 0.0), writes=[bc[:, :]])
    for c in range(4):
        for p in range(3):
            for l in range(4):
                pr = prodb[:, (l % 2) * 512:(l % 2) * 512 + 512]
                ctx.op(V, lambda e, c=c, p=p, l=l, pr=pr: e.tensor_tensor(
                    pr, q[:, c * T + l * 512: c * T + l * 512 + 512],
                    k2[:, (p * 4 + c) * 512: (p * 4 + c) * 512 + 512], ALU.mult),
                    reads=[q[:, c * T + l * 512: c * T + l * 512 + 512],
                           k2[:, (p * 4 + c) * 512: (p * 4 + c) * 512 + 512]],
                    writes=[pr])
                opid = ctx.op("tensor", lambda e, pr=pr, l=l: e.matmul(
                    A[0:4, l * 512: l * 512 + 512], e4b[:, :], pr, start=True, stop=True),
                    reads=[e4b[:, :], pr], war_writes=[A[0:4, l * 512: l * 512 + 512]],
                    provide=False)
                ctx.record_write(A[0:4, l * 512: l * 512 + 512], opid)
            ctx.op(V, lambda e, c=c, p=p: e.tensor_copy(
                acp[0:4, p * T:(p + 1) * T], A[0:4, 0:T]),
                reads=[A[0:4, 0:T]], writes=[acp[0:4, p * T:(p + 1) * T]])
        mrow = Bb[0:4, 0:T]
        ctx.op(V, lambda e: e.tensor_tensor(mrow, acp[0:4, 0:T], acp[0:4, T:2 * T], ALU.max),
               reads=[acp[0:4, 0:T], acp[0:4, T:2 * T]], writes=[mrow])
        ctx.op(V, lambda e: e.tensor_tensor(mrow, mrow, acp[0:4, 2 * T:3 * T], ALU.max),
               reads=[mrow, acp[0:4, 2 * T:3 * T]], writes=[mrow])
        for p in range(3):
            sub = A[0:4, 0:T]
            ctx.op(V, lambda e, p=p, sub=sub: e.tensor_tensor(
                sub, acp[0:4, p * T:(p + 1) * T], mrow, ALU.subtract),
                reads=[acp[0:4, p * T:(p + 1) * T], mrow], writes=[sub])
            ctx.op("scalar", lambda e, p=p, sub=sub: e.activation(
                acp[0:4, p * T:(p + 1) * T], sub, AF.Exp, scale=0.125),
                reads=[sub], writes=[acp[0:4, p * T:(p + 1) * T]])
        den = A[0:4, 0:T]
        ctx.op(V, lambda e: e.tensor_tensor(den, acp[0:4, 0:T], acp[0:4, T:2 * T], ALU.add),
               reads=[acp[0:4, 0:T], acp[0:4, T:2 * T]], writes=[den])
        ctx.op(V, lambda e: e.tensor_tensor(den, den, acp[0:4, 2 * T:3 * T], ALU.add),
               reads=[den, acp[0:4, 2 * T:3 * T]], writes=[den])
        rec = Bb[0:4, 0:T]
        ctx.op(V, lambda e: e.reciprocal(rec, den), reads=[den], writes=[rec])
        for p in range(3):
            ctx.op(V, lambda e, p=p: e.tensor_tensor(
                acp[0:4, p * T:(p + 1) * T], acp[0:4, p * T:(p + 1) * T], rec, ALU.mult),
                reads=[acp[0:4, p * T:(p + 1) * T], rec],
                writes=[acp[0:4, p * T:(p + 1) * T]])
        for p in range(3):
            for l in range(4):
                pt = Bb[:, ((p * 4 + l) % 2 + 2) * 512: ((p * 4 + l) % 2 + 2) * 512 + 512]
                mm_group(ctx, pt, [(selc[:, c * 128:(c + 1) * 128],
                                    acp[0:4, p * T + l * 512: p * T + l * 512 + 512])])
                dst = bc[:, (p * 4 + l) * 512: (p * 4 + l) * 512 + 512]
                ctx.op(V, lambda e, pt=pt, dst=dst: e.tensor_tensor(
                    dst, dst, pt, ALU.add),
                    reads=[dst, pt], writes=[dst])

    for p in range(3):
        for vc in range(NCH):
            vt = vv[:, (p * NCH + vc) * 512: (p * NCH + vc) * 512 + 512]
            ctx.op(V, lambda e, p=p, vc=vc, vt=vt: e.tensor_scalar(
                vt, xn[:, vc * T: vc * T + 512], wpb[:, p:p + 1], None, ALU.mult),
                reads=[xn[:, vc * T: vc * T + 512], wpb[:, p:p + 1]], writes=[vt])
            for l in range(1, 4):
                ctx.op(V, lambda e, p=p, vc=vc, l=l, vt=vt: e.scalar_tensor_tensor(
                    vt, xn[:, vc * T + l * 512: vc * T + l * 512 + 512],
                    wpb[:, l * 3 + p: l * 3 + p + 1], vt, ALU.mult, ALU.add),
                    reads=[xn[:, vc * T + l * 512: vc * T + l * 512 + 512],
                           wpb[:, l * 3 + p: l * 3 + p + 1], vt],
                    writes=[vt])

    for l in range(4):
        for vc in range(NCH):
            ot = ol[:, vc * 512:(vc + 1) * 512]
            ctx.op(V, lambda e, l=l, vc=vc, ot=ot: e.tensor_tensor(
                ot, vv[:, (0 * NCH + vc) * 512: (0 * NCH + vc) * 512 + 512],
                bc[:, (0 * 4 + l) * 512: (0 * 4 + l) * 512 + 512], ALU.mult),
                reads=[vv[:, vc * 512: vc * 512 + 512],
                       bc[:, l * 512: l * 512 + 512]], writes=[ot])
            for p in range(1, 3):
                pr = prodb[:, 0:512]
                ctx.op(V, lambda e, p=p, l=l, vc=vc, pr=pr: e.tensor_tensor(
                    pr, vv[:, (p * NCH + vc) * 512: (p * NCH + vc) * 512 + 512],
                    bc[:, (p * 4 + l) * 512: (p * 4 + l) * 512 + 512], ALU.mult),
                    reads=[vv[:, (p * NCH + vc) * 512: (p * NCH + vc) * 512 + 512],
                           bc[:, (p * 4 + l) * 512: (p * 4 + l) * 512 + 512]],
                    writes=[pr])
                ctx.op(V, lambda e, ot=ot, pr=pr: e.tensor_tensor(ot, ot, pr, ALU.add),
                       reads=[ot, pr], writes=[ot])
        ldx(l)
        for j in range(NCH):
            wb_ = wbuf if j % 2 == 0 else wbuf2
            ctx.dma("gpsimd", wb_.rearrange("p (k o) -> p k o", o=128),
                    wot_d.rearrange("(k p) o -> p k o", p=128)[:, :, j * 128:(j + 1) * 128])
            pt = Bb[:, (j % 4) * 512:(j % 4) * 512 + 512]
            parts = [(wb_[:, k * 128:(k + 1) * 128],
                      ol[:, k * 512:(k + 1) * 512]) for k in range(NCH)]
            parts.append((bor[0:1, j * 128:(j + 1) * 128], sc["ones_rowb"][0:1, 0:512]))
            mm_group(ctx, pt, parts)
            yb = ybuf[:, (j % 2) * 512:(j % 2) * 512 + 512]
            ctx.op(V, lambda e, pt=pt, yb=yb, j=j, l=l: e.scalar_tensor_tensor(
                yb, pt, 1.0, xqs(j, l), ALU.mult, ALU.add),
                reads=[pt, xqs(j, l)], writes=[yb])
            st = ctx.dma("sync",
                         y_d.rearrange("(c p) t -> p c t", p=128)[:, j, l * 512:(l + 1) * 512],
                         yb)
            ctx.barrier_output(st)

    ctx.finalize()
    return nc


def build_l0_attn(nc, es, debug=False):
    ctx = Ctx(nc, es)
    x_d = nc.dram_tensor("x_fm", [D, T], F32, kind="ExternalInput").ap()
    wq_d = nc.dram_tensor("wq", [D, D], F32, kind="ExternalInput").ap()
    wk_d = nc.dram_tensor("wk", [D, D], F32, kind="ExternalInput").ap()
    wv_d = nc.dram_tensor("wv", [D, D], F32, kind="ExternalInput").ap()
    bq_d = nc.dram_tensor("bq64", [64, 16], F32, kind="ExternalInput").ap()
    bk_d = nc.dram_tensor("bk64", [64, 16], F32, kind="ExternalInput").ap()
    bv_d = nc.dram_tensor("bv64", [64, 16], F32, kind="ExternalInput").ap()
    wo_d = nc.dram_tensor("wo_sl", [64, 16 * D], F32, kind="ExternalInput").ap()
    bo_d = nc.dram_tensor("bo_r", [1, D], BF16, kind="ExternalInput").ap()
    idb_d = nc.dram_tensor("identb", [128, 128], BF16, kind="ExternalInput").ap()
    y_d = nc.dram_tensor("y_fm", [D, T], F32, kind="ExternalOutput").ap()
    if debug:
        dbg_d = nc.dram_tensor("dbg", [128, 208], F32, kind="ExternalOutput").ap()

    xq = ctx.sbuf("xq", [128, NCH * 512], F32)
    xn = ctx.sbuf("xn", [128, NCH * T], BF16)
    ybuf = ctx.sbuf("ybuf", [128, NCH * 512], F32)
    qu = [ctx.sbuf("qu%d" % j, [64, 4096], BF16) for j in range(2)]
    ku = [ctx.sbuf("ku%d" % j, [64, 4096], BF16) for j in range(2)]
    vu = [ctx.sbuf("vu%d" % j, [64, 4096], BF16) for j in range(2)]
    ao2 = [ctx.sbuf("ao2%d" % j, [64, 4096], BF16) for j in range(2)]
    esc = ctx.sbuf("esc", [128, 2 * 512], BF16)
    dacc = ctx.sbuf("dacc", [128, 16], F32)
    den = ctx.sbuf("den", [128, 2 * 32], F32)
    sdg = ctx.sbuf("sdg", [128, 2 * 32], F32)
    dA = ctx.sbuf("dA", [128, 2 * 32], F32)
    prodt = ctx.sbuf("prodt", [64, 2 * 128], BF16)
    aot = ctx.sbuf("aot", [128, 2 * 64], BF16)
    b64 = ctx.sbuf("b64", [64, 3 * 16], F32)
    bo_r = ctx.sbuf("bo_rs", [1, D], BF16)
    wbuf3 = ctx.sbuf("wbuf3", [128, 2 * NCH * 64], BF16)
    wbuf4 = ctx.sbuf("wbuf4", [64, 2 * 16 * 128], BF16)
    identb = ctx.sbuf("identbs", [128, 128], BF16)
    dbgb = ctx.sbuf("dbgb", [128, 208], F32)
    if debug:
        ctx.op("vector", lambda e: e.memset(dbgb[:, :], 0.0), writes=[dbgb[:, :]])
    sc = make_common(ctx, rows_tokens=T, arena_b=1024)
    C = ctx.psum("psC", [128, 2048], BF16)
    A = sc["arenaA"]
    Bp = sc["arenaB"]
    V = "vector"

    ctx.dma("sync", b64[:, 0:16], bq_d)
    ctx.dma("sync", b64[:, 16:32], bk_d)
    ctx.dma("sync", b64[:, 32:48], bv_d)
    ctx.dma("sync", bo_r[:, :], bo_d)
    ctx.dma("sync", identb[:, :], idb_d)

    def ldx(tc):
        tce = tc % NTC
        ctx.dma("sync", xq.rearrange("p (c t) -> p c t", c=NCH),
                x_d.rearrange("(c p) t -> p c t", p=128)[:, :, tce * 512:(tce + 1) * 512])

    def xqs(k, tc):
        return xq[:, k * 512:(k + 1) * 512]

    emit_ln_stats(ctx, xqs, sc, xn, sq_slots=NCH * NTC, pre_tc=ldx)
    emit_ln_apply(ctx, xqs,
                  (lambda k, i: xn[:, k * T + i * 512: k * T + i * 512 + 512]),
                  sc, list(range(NTC)), pre_tc=ldx)
    if debug:
        RT = sc["rows_tokens"]
        for (lo, w_, src_) in [(24, 8, sc["rows"][0:1, 0 * RT:0 * RT + 8]),
                               (32, 8, sc["rows"][0:1, 1 * RT:1 * RT + 8]),
                               (40, 8, sc["rows"][0:1, 2 * RT:2 * RT + 8]),
                               (48, 8, xn[:, 0:8])]:
            ctx.op("vector", lambda e, lo=lo, w_=w_, src_=src_: e.tensor_copy(
                dbgb[0:1, lo:lo + w_] if src_.ap[0][1] == 1 else dbgb[:, lo:lo + w_], src_),
                reads=[src_], writes=[dbgb[:, lo:lo + w_]])

    SCL = 1.0 / 32.0
    for n in range(B):
        for pi, (wd, dst, bcol, scl2) in enumerate(
                [(wq_d, qu, 0, SCL), (wk_d, ku, 16, SCL), (wv_d, vu, 32, None)]):
            for jj in range(16):
                wb_ = wbuf3[:, (jj % 2) * 512: (jj % 2) * 512 + 512]
                ctx.dma("gpsimd", wb_.rearrange("p (k o) -> p k o", o=64),
                        wd.rearrange("(k p) o -> p k o", p=128)[:, :, jj * 64:(jj + 1) * 64])
                pt = Bp[0:64, (jj % 2) * 512: (jj % 2) * 512 + 512]
                mm_group(ctx, pt, [(wb_[:, k * 64:(k + 1) * 64],
                                    xn[:, k * T + n * 512: k * T + n * 512 + 512])
                                   for k in range(NCH)])
                if debug and n == 0 and pi == 0 and jj == 0:
                    ctx.op("vector", lambda e, pt=pt: e.tensor_copy(
                        dbgb[0:64, 0:8], pt[:, 0:8]), reads=[pt], writes=[dbgb[0:64, 0:8]])
                    ctx.op("vector", lambda e, wb_=wb_: e.tensor_copy(
                        dbgb[:, 8:16], wb_[:, 0:8]), reads=[wb_[:, 0:8]], writes=[dbgb[:, 8:16]])
                    ctx.op("vector", lambda e: e.tensor_copy(
                        dbgb[:, 16:24], xn[:, n * 512: n * 512 + 8]),
                        reads=[xn[:, n * 512: n * 512 + 8]], writes=[dbgb[:, 16:24]])
                for j in range(2):
                    dstap = bass.AP(dst[j], jj, [[4096, 64], [16, 256]])
                    src = pt[:, j * 256:(j + 1) * 256]
                    if scl2 is None:
                        ctx.op(V, lambda e, dstap=dstap, src=src, jj=jj, bcol=bcol: e.tensor_scalar(
                            dstap, src, b64[:, bcol + jj: bcol + jj + 1], None, ALU.add),
                            reads=[src, b64[:, bcol + jj: bcol + jj + 1]],
                            writes=[dst[j][0:64, :]])
                    else:
                        ctx.op(V, lambda e, dstap=dstap, src=src, jj=jj, bcol=bcol,
                               scl2=scl2: e.tensor_scalar(
                            dstap, src, b64[:, bcol + jj: bcol + jj + 1], scl2, ALU.add, ALU.mult),
                            reads=[src, b64[:, bcol + jj: bcol + jj + 1]],
                            writes=[dst[j][0:64, :]])
        for j in range(2):
            for bt in range(32):
                for ac in range(8):
                    pt = A[:, (ac % 4) * 512: (ac % 4) * 512 + 512]
                    opid = ctx.op("tensor", lambda e, j=j, bt=bt, ac=ac, pt=pt: e.matmul(
                        pt, ku[j][0:64, bt * 128: bt * 128 + 128],
                        qu[j][0:64, ac * 512: ac * 512 + 512], start=True, stop=True),
                        reads=[ku[j][0:64, bt * 128: bt * 128 + 128],
                               qu[j][0:64, ac * 512: ac * 512 + 512]],
                        war_writes=[pt], provide=False)
                    ctx.record_write(pt, opid)
                    ctx.op("scalar", lambda e, pt=pt, ac=ac: e.activation(
                        esc[:, (ac % 2) * 512: (ac % 2) * 512 + 512], pt, AF.Exp,
                        accum_out=dacc[:, ac:ac + 1]),
                        reads=[pt],
                        writes=[esc[:, (ac % 2) * 512: (ac % 2) * 512 + 512],
                                dacc[:, ac:ac + 1]])
                    if debug and n == 0 and j == 0 and bt == 0 and ac == 0:
                        ctx.op("vector", lambda e, pt=pt: e.tensor_copy(
                            dbgb[:, 100:120], pt[:, 0:20]), reads=[pt],
                            writes=[dbgb[:, 100:120]])
                        ctx.op("vector", lambda e: e.tensor_copy(
                            dbgb[0:64, 80:100], qu[0][0:64, 0:20]), reads=[qu[0][0:64, 0:20]],
                            writes=[dbgb[0:64, 80:100]])
                        ctx.op("vector", lambda e: e.tensor_copy(
                            dbgb[0:64, 60:80], ku[0][0:64, 0:20]), reads=[ku[0][0:64, 0:20]],
                            writes=[dbgb[0:64, 60:80]])
                ctx.op(V, lambda e, j=j, bt=bt: e.reduce_sum(
                    den[:, j * 32 + bt: j * 32 + bt + 1], dacc[:, 0:8], axis=AX.X),
                    reads=[dacc[:, 0:8]], writes=[den[:, j * 32 + bt: j * 32 + bt + 1]])
            for bt in range(32):
                pr = prodt[0:64, (bt % 2) * 128: (bt % 2) * 128 + 128]
                ctx.op(V, lambda e, j=j, bt=bt, pr=pr: e.tensor_tensor(
                    pr, qu[j][0:64, bt * 128: bt * 128 + 128],
                    ku[j][0:64, bt * 128: bt * 128 + 128], ALU.mult),
                    reads=[qu[j][0:64, bt * 128: bt * 128 + 128],
                           ku[j][0:64, bt * 128: bt * 128 + 128]],
                    writes=[pr])
                ct = C[:, 0:64]
                opid = ctx.op("tensor", lambda e, pr=pr, ct=ct: e.transpose(
                    ct, pr, identb[0:64, 0:64]),
                    reads=[pr, identb[0:64, 0:64]], war_writes=[ct], provide=False)
                ctx.record_write(ct, opid)
                ctx.op(V, lambda e, j=j, bt=bt, ct=ct: e.reduce_sum(
                    sdg[:, j * 32 + bt: j * 32 + bt + 1], ct, axis=AX.X),
                    reads=[ct], writes=[sdg[:, j * 32 + bt: j * 32 + bt + 1]])
            ctx.op("scalar", lambda e, j=j: e.activation(
                sdg[:, j * 32: j * 32 + 32], sdg[:, j * 32: j * 32 + 32], AF.Exp),
                reads=[sdg[:, j * 32: j * 32 + 32]], writes=[sdg[:, j * 32: j * 32 + 32]])
            ctx.op(V, lambda e, j=j: e.reciprocal(
                dA[:, j * 32: j * 32 + 32], den[:, j * 32: j * 32 + 32]),
                reads=[den[:, j * 32: j * 32 + 32]], writes=[dA[:, j * 32: j * 32 + 32]])
            ctx.op(V, lambda e, j=j: e.tensor_tensor(
                dA[:, j * 32: j * 32 + 32], dA[:, j * 32: j * 32 + 32],
                sdg[:, j * 32: j * 32 + 32], ALU.mult),
                reads=[dA[:, j * 32: j * 32 + 32], sdg[:, j * 32: j * 32 + 32]],
                writes=[dA[:, j * 32: j * 32 + 32]])
            if debug and n == 0 and j == 0:
                for (lo, w_, src_) in [(0, 32, den[:, 0:32]), (64, 32, sdg[:, 0:32]),
                                       (128, 32, dA[:, 0:32]), (192, 8, dacc[:, 0:8])]:
                    ctx.op("vector", lambda e, lo=lo, w_=w_, src_=src_: e.tensor_copy(
                        dbgb[:, lo:lo + w_], src_), reads=[src_], writes=[dbgb[:, lo:lo + w_]])
            for bt in range(32):
                ct = C[:, 1024:1024 + 64]
                opid = ctx.op("tensor", lambda e, j=j, bt=bt, ct=ct: e.transpose(
                    ct, vu[j][0:64, bt * 128: bt * 128 + 128], identb[0:64, 0:64]),
                    reads=[vu[j][0:64, bt * 128: bt * 128 + 128], identb[0:64, 0:64]],
                    war_writes=[ct], provide=False)
                ctx.record_write(ct, opid)
                at = aot[:, (bt % 2) * 64: (bt % 2) * 64 + 64]
                ctx.op(V, lambda e, j=j, bt=bt, ct=ct, at=at: e.tensor_scalar(
                    at, ct, dA[:, j * 32 + bt: j * 32 + bt + 1], None, ALU.mult),
                    reads=[ct, dA[:, j * 32 + bt: j * 32 + bt + 1]], writes=[at])
                c2 = C[0:64, 0:128]
                opid = ctx.op("tensor", lambda e, at=at, c2=c2: e.transpose(
                    c2, at, identb[:, :]),
                    reads=[at, identb[:, :]], war_writes=[c2], provide=False)
                ctx.record_write(c2, opid)
                dstap = bass.AP(ao2[j], 8 * bt, [[4096, 64], [256, 16], [1, 8]])
                srcap = bass.AP(C, 0, [[2048, 64], [1, 16], [16, 8]])
                ctx.op(V, lambda e, srcap=srcap, dstap=dstap, c2=c2: e.tensor_copy(
                    dstap, srcap),
                    reads=[c2], writes=[ao2[j][0:64, :]])
        ldx(n)
        for och in range(NCH):
            wb_ = wbuf4[:, (och % 2) * 2048: (och % 2) * 2048 + 2048]
            ctx.dma("gpsimd", wb_.rearrange("p (h o) -> p h o", h=16),
                    wo_d.rearrange("p (h o) -> p h o", h=16)[:, :, och * 128:(och + 1) * 128])
            for j in range(2):
                pt = Bp[:, (j % 2) * 512: (j % 2) * 512 + 256]
                parts = [(wb_[:, h * 128:(h + 1) * 128],
                          ao2[j][0:64, h * 256:(h + 1) * 256]) for h in range(16)]
                parts.append((bo_r[0:1, och * 128:(och + 1) * 128],
                              sc["ones_rowb"][0:1, 0:256]))
                mm_group(ctx, pt, parts)
                yb = ybuf[:, och * 512 + j * 256: och * 512 + j * 256 + 256]
                ctx.op(V, lambda e, pt=pt, yb=yb, och=och, j=j: e.scalar_tensor_tensor(
                    yb, pt, 1.0, xq[:, och * 512 + j * 256: och * 512 + j * 256 + 256],
                    ALU.mult, ALU.add),
                    reads=[pt, xq[:, och * 512 + j * 256: och * 512 + j * 256 + 256]],
                    writes=[yb])
        st = ctx.dma("sync",
                     y_d.rearrange("(c p) t -> p c t", p=128)[:, :, n * 512:(n + 1) * 512],
                     ybuf.rearrange("p (c t) -> p c t", c=NCH))
        ctx.barrier_output(st)
    if debug:
        st2 = ctx.dma("sync", dbg_d[:, :], dbgb[:, :])
        ctx.barrier_output(st2)

    ctx.finalize()
    return nc


def build_l2_attn(nc, es):
    ctx = Ctx(nc, es)
    xb_d = nc.dram_tensor("xb_fm", [D, S], F32, kind="ExternalInput").ap()
    wq_d = nc.dram_tensor("wq_cat", [D, 512], F32, kind="ExternalInput").ap()
    wk_d = nc.dram_tensor("wk_cat", [D, 512], F32, kind="ExternalInput").ap()
    wv_d = nc.dram_tensor("wv_cat", [D, 512], F32, kind="ExternalInput").ap()
    wo_d = nc.dram_tensor("wout", [D, D], F32, kind="ExternalInput").ap()
    cm_d = nc.dram_tensor("cmaskT", [4, 128, 512], BF16, kind="ExternalInput").ap()
    slab_d = nc.dram_tensor("slab", [S, 512], BF16, kind="ExternalOutput").ap()
    ws_d = nc.dram_tensor("wsum", [128, NCH], F32, kind="ExternalOutput").ap()

    NQ = S // 512
    qf = ctx.sbuf("qf", [128, 4 * S], BF16)
    kf = ctx.sbuf("kf", [128, 4 * S], BF16)
    vt = ctx.sbuf("vt", [128, 32 * 520], BF16)
    xbq = ctx.sbuf("xbq", [128, NCH * 1024], BF16)
    xnq = ctx.sbuf("xnq", [128, NCH * 1024], BF16)
    pbuf = ctx.sbuf("pbuf", [128, 3 * 512], BF16)
    aon = ctx.sbuf("aon", [64, S], BF16)
    aot2 = [ctx.sbuf("aot2%d" % i, [64, 2048], BF16) for i in range(2)]
    cmask = ctx.sbuf("cmask", [128, 4 * 512], BF16)
    bc = ctx.sbuf("bcx", [64, 512], BF16)
    wsums = ctx.sbuf("wsumsb", [128, NCH], F32)
    wbuf = ctx.sbuf("wbuf", [128, NCH * 512], BF16)
    wbuf2 = ctx.sbuf("wbuf2", [128, NCH * 512], BF16)
    sc = make_common(ctx, rows_tokens=1024)
    recr = sc["rows"][:, 1024:2048]
    A = sc["arenaA"]
    Bb = sc["arenaB"]
    V = "vector"

    ctx.dma("sync", cmask.rearrange("p (i q) -> p i q", i=4),
            cm_d.rearrange("i p q -> p i q"))
    ctx.op("gpsimd", lambda e: e.memset(
        vt.rearrange("p (i h o) -> p i h o", i=32, h=8)[:, :, :, 64:65], 1.0),
        writes=[vt[:, :]])

    for j in range(NCH):
        ctx.dma("gpsimd", wbuf.rearrange("p (k o) -> p k o", o=128)[:, :, 0:128]
                if False else wbuf[:, 0:NCH * 128].rearrange("p (k o) -> p k o", o=128),
                wo_d.rearrange("(k p) o -> p k o", p=128)[:, :, j * 128:(j + 1) * 128])
        pt = A[:, (j % 2) * 512: (j % 2) * 512 + 1]
        mm_group(ctx, pt, [(wbuf[:, k * 128:(k + 1) * 128], sc["ones_colb"][:, :])
                           for k in range(NCH)])
        ctx.op(V, lambda e, j=j, pt=pt: e.tensor_copy(wsums[:, j:j + 1], pt),
               reads=[pt], writes=[wsums[:, j:j + 1]])
    stw = ctx.dma("sync", ws_d, wsums[:, :])
    ctx.barrier_output(stw)

    for tq in range(4):
        ctx.dma("gpsimd", xbq.rearrange("p (c t) -> p c t", c=NCH),
                xb_d.rearrange("(c p) t -> p c t", p=128)[:, :, tq * 1024:(tq + 1) * 1024])
        emit_ln(ctx, xbq, xnq, sc, tokens=1024, ones_key="ones_colb")
        for pi, (wd, dstf, scl) in enumerate([(wq_d, qf, 0.125), (wk_d, kf, None)]):
            ctx.dma("gpsimd", (wbuf if pi == 0 else wbuf2).rearrange("p (k o) -> p k o", o=512),
                    wd.rearrange("(k p) o -> p k o", p=128))
            wb_ = wbuf if pi == 0 else wbuf2
            for oc in range(4):
                for qc in range(2):
                    pt = Bb[:, ((oc * 2 + qc) % 4) * 512: ((oc * 2 + qc) % 4) * 512 + 512]
                    mm_group(ctx, pt, [(wb_[:, k * 512 + oc * 128: k * 512 + oc * 128 + 128],
                                        xnq[:, k * 1024 + qc * 512: k * 1024 + qc * 512 + 512])
                                       for k in range(NCH)])
                    dst = dstf[:, oc * S + tq * 1024 + qc * 512: oc * S + tq * 1024 + qc * 512 + 512]
                    if scl is None:
                        ctx.op("scalar", lambda e, pt=pt, dst=dst: e.activation(
                            dst, pt, AF.Identity), reads=[pt], writes=[dst])
                    else:
                        ctx.op("scalar", lambda e, pt=pt, dst=dst, scl=scl: e.activation(
                            dst, pt, AF.Identity, scale=scl), reads=[pt], writes=[dst])
        ctx.dma("gpsimd", wbuf.rearrange("p (k o) -> p k o", o=512),
                wv_d.rearrange("(k p) o -> p k o", p=128))
        for kt8 in range(8):
            kt = tq * 8 + kt8
            pt = Bb[:, (kt8 % 4) * 512: (kt8 % 4) * 512 + 512]
            mm_group(ctx, pt, [(xnq[:, k * 1024 + kt8 * 128: k * 1024 + kt8 * 128 + 128],
                                wbuf[:, k * 512:(k + 1) * 512]) for k in range(NCH)])
            dsts = vt.rearrange("p (i h o) -> p i h o", i=32, h=8)[:, kt, :, 0:64]
            ctx.op(V, lambda e, pt=pt, dsts=dsts: e.tensor_copy(
                dsts, pt.rearrange("p (h o) -> p h o", h=8)),
                reads=[pt], writes=[dsts])

    for n in range(8):
        for qs in range(NQ):
            pav = A[0:65, (qs % 2) * 1024: (qs % 2) * 1024 + 512]
            nkt = 4 * qs + 4
            for ti in range(nkt):
                pt = Bb[:, (ti % 2) * 512: (ti % 2) * 512 + 512]
                lh = kf[(n % 2) * 64:(n % 2) * 64 + 64,
                        (n // 2) * S + ti * 128: (n // 2) * S + ti * 128 + 128]
                rh = qf[(n % 2) * 64:(n % 2) * 64 + 64,
                        (n // 2) * S + qs * 512: (n // 2) * S + qs * 512 + 512]
                opid = ctx.op("tensor", lambda e, pt=pt, lh=lh, rh=rh: e.matmul(
                    pt, lh, rh, start=True, stop=True),
                    reads=[lh, rh], war_writes=[pt], provide=False)
                ctx.record_write(pt, opid)
                ps = pbuf[:, (ti % 3) * 512: (ti % 3) * 512 + 512]
                ctx.op("scalar", lambda e, pt=pt, ps=ps: e.activation(ps, pt, AF.Exp),
                       reads=[pt], writes=[ps])
                m = ti - 4 * qs
                if m >= 0:
                    ctx.op(V, lambda e, ps=ps, m=m: e.tensor_tensor(
                        ps, ps, cmask[:, m * 512:(m + 1) * 512], ALU.mult),
                        reads=[ps, cmask[:, m * 512:(m + 1) * 512]], writes=[ps])
                lhv = vt.rearrange("p (i c) -> p i c", i=32)[:, ti, n * 65: n * 65 + 65]
                if ti == 0:
                    opid = ctx.op("tensor", lambda e, pav=pav, lhv=lhv, ps=ps: e.matmul(
                        pav, lhv, ps, start=True, stop=(nkt == 1)),
                        reads=[lhv, ps], war_writes=[pav], provide=False)
                elif ti == nkt - 1:
                    opid = ctx.op("tensor", lambda e, pav=pav, lhv=lhv, ps=ps: e.matmul(
                        pav, lhv, ps, start=False, stop=True),
                        reads=[lhv, ps])
                    ctx.record_write(pav, opid)
                else:
                    ctx.op("tensor", lambda e, pav=pav, lhv=lhv, ps=ps: e.matmul(
                        pav, lhv, ps, start=False, stop=False),
                        reads=[lhv, ps])
            ctx.op(V, lambda e, pav=pav: e.reciprocal(recr[0:1, 0:512], pav[64:65, :]),
                   reads=[pav[64:65, :]], writes=[recr[0:1, 0:512]])
            pb2 = Bb[0:64, 1024:1024 + 512]
            mm_group(ctx, pb2, [(sc["ones_row"][0:1, 0:64], recr[0:1, 0:512])])
            ctx.op(V, lambda e, pb2=pb2: e.tensor_copy(bc[:, :], pb2),
                   reads=[pb2], writes=[bc[:, :]])
            dst = aon[0:64, qs * 512:(qs + 1) * 512]
            ctx.op(V, lambda e, pav=pav, dst=dst: e.tensor_tensor(
                dst, pav[0:64, :], bc[:, :], ALU.mult),
                reads=[pav[0:64, :], bc[:, :]], writes=[dst])
        for par in range(2):
            srcap = bass.AP(aon, par, [[S, 64], [2, 32], [64, 64]])
            dstap = bass.AP(aot2[par], 0, [[2048, 64], [64, 32], [1, 64]])
            ctx.op(V, lambda e, srcap=srcap, dstap=dstap: e.tensor_copy(dstap, srcap),
                   reads=[aon[0:64, :]], writes=[aot2[par][0:64, :]])
            dstd = bass.AP(slab_d.tensor, 64 * n + par * 64 * 512,
                           [[512, 64], [2 * 64 * 512, 32], [1, 64]])
            srcd = bass.AP(aot2[par], 0, [[2048, 64], [64, 32], [1, 64]])
            st = ctx.dma("gpsimd", dstd, srcd)
            ctx.barrier_output(st)

    ctx.finalize()
    return nc



import ml_dtypes

_BUILDS = {}


def _program(name, builder):
    if name not in _BUILDS:
        es = ExitStack()
        nc = bass.Bass("TRN2", target_bir_lowering=False)
        es.__enter__()
        builder(nc, es)
        _BUILDS[name] = (nc, es)
    return _BUILDS[name][0]


def _colscales(s):
    n = s.shape[0] // 128
    return np.ascontiguousarray(np.asarray(s, np.float32).reshape(n, 128).T)


def _run(name, builder, in_maps):
    nc = _program(name, builder)
    return run_bass_kernel_spmd(nc, in_maps, core_ids=list(range(N_CORES)))


def kernel(**inp):
    f32 = np.float32
    bf16 = ml_dtypes.bfloat16
    x = np.asarray(inp["L0_attn_X_embed"], f32)
    xf = _fm_shard(x)

    wo = np.asarray(inp["L0_attn_W_o"], f32)
    wmap = {
        "wq": np.asarray(inp["L0_attn_W_q"], f32),
        "wk": np.asarray(inp["L0_attn_W_k"], f32),
        "wv": np.asarray(inp["L0_attn_W_v"], f32),
        "wo_sl": np.ascontiguousarray(wo.reshape(16, 64, 1024).transpose(1, 0, 2).reshape(64, 16384)),
        "bq64": np.ascontiguousarray(np.asarray(inp["L0_attn_b_q"], f32).reshape(16, 64).T),
        "bk64": np.ascontiguousarray(np.asarray(inp["L0_attn_b_k"], f32).reshape(16, 64).T),
        "bv64": np.ascontiguousarray(np.asarray(inp["L0_attn_b_v"], f32).reshape(16, 64).T),
        "bo_r": np.asarray(inp["L0_attn_b_o"], f32).reshape(1, 1024).astype(bf16),
        "identb": np.eye(128, dtype=bf16),
    }
    res = _run("l0a", build_l0_attn, [dict(x_fm=xf[c], **wmap) for c in range(N_CORES)])
    xf = [res.results[c]["y_fm"] for c in range(N_CORES)]

    wmap = {
        "w1t": np.ascontiguousarray(np.asarray(inp["L0_ffn_w1_w"], f32).T),
        "w2t": np.ascontiguousarray(np.asarray(inp["L0_ffn_w2_w"], f32).T),
        "h1t": np.ascontiguousarray(np.asarray(inp["L0_ffn_h1_w"], f32).T),
        "h2t": np.ascontiguousarray(np.asarray(inp["L0_ffn_h2_w"], f32).T),
        "w3t": np.ascontiguousarray(np.asarray(inp["L0_ffn_w3_w"], f32).T),
        "w1s": _colscales(inp["L0_ffn_w1_s"]), "w2s": _colscales(inp["L0_ffn_w2_s"]),
        "h1s": _colscales(inp["L0_ffn_h1_s"]), "h2s": _colscales(inp["L0_ffn_h2_s"]),
        "w3s": _colscales(inp["L0_ffn_w3_s"]),
    }
    res = _run("l0f", build_l0_ffn, [dict(x_fm=xf[c], **wmap) for c in range(N_CORES)])
    xf = [res.results[c]["y_fm"] for c in range(N_CORES)]

    w_kv = np.asarray(inp["L1_attn_w_kv"], f32)
    wp = np.zeros((1, 12), f32)
    for l in range(4):
        for p in range(3):
            wp[0, l * 3 + p] = w_kv[p, l]
    perm = np.zeros((512, 512), f32)
    for h in range(16):
        for d2 in range(32):
            perm[32 * h + d2, 16 * d2 + h] = 1.0
    e4b = np.zeros((128, 4), f32)
    for r in range(128):
        e4b[r, r // 32] = 1.0
    selc = np.zeros((4, 4 * 128), f32)
    for cc in range(4):
        for hh in range(4):
            for r in range(128):
                if r % 16 == 4 * cc + hh:
                    selc[hh, cc * 128 + r] = 1.0
    wmap = {
        "wqt": np.ascontiguousarray(np.asarray(inp["L1_attn_w_q"], f32).T),
        "bq_c": np.ascontiguousarray(np.asarray(inp["L1_attn_b_q"], f32).reshape(4, 128).T),
        "wp": wp, "permT": np.ascontiguousarray(perm.T).astype(bf16),
        "e4b": e4b.astype(bf16), "selc": selc.astype(bf16),
        "wot": np.ascontiguousarray(np.asarray(inp["L1_attn_w_out"], f32).T),
        "bo_r": np.asarray(inp["L1_attn_b_out"], f32).reshape(1, 1024).astype(bf16),
    }
    res = _run("l1a", build_l1_attn, [dict(x_fm=xf[c], **wmap) for c in range(N_CORES)])
    xf = [res.results[c]["y_fm"] for c in range(N_CORES)]

    wmap = {"w_t": np.ascontiguousarray(np.asarray(inp["L1_ffn_weight"], f32).T)}
    res = _run("l1f", build_l1_ffn, [dict(x_fm=xf[c], **wmap) for c in range(N_CORES)])
    xf = [res.results[c]["y_fm"] for c in range(N_CORES)]

    W_qkv = np.asarray(inp["L2_attn_W_qkv"], f32)
    W_out = np.asarray(inp["L2_attn_W_out"], f32)
    cmaskT = np.zeros((4, 128, 512), f32)
    for mm in range(4):
        for kr in range(128):
            cmaskT[mm, kr, 128 * mm + kr:] = 1.0
    xfull = _fm_unshard(xf)
    maps = []
    for c in range(N_CORES):
        b0, g = c // 2, c % 2
        heads = list(range(8 * g, 8 * g + 8))
        maps.append({
            "xb_fm": np.ascontiguousarray(xfull[b0].T),
            "wq_cat": np.ascontiguousarray(np.concatenate([W_qkv[n][:, 0:64] for n in heads], axis=1)),
            "wk_cat": np.ascontiguousarray(np.concatenate([W_qkv[n][:, 64:128] for n in heads], axis=1)),
            "wv_cat": np.ascontiguousarray(np.concatenate([W_qkv[n][:, 128:192] for n in heads], axis=1)),
            "wout": np.ascontiguousarray(W_out.reshape(1024, 1024)),
            "cmaskT": cmaskT.astype(bf16),
        })
    res = _run("l2a", build_l2_attn, maps)
    wsum = res.results[0]["wsum"]
    xattn = np.zeros((B, S, D), dtype=bf16)
    for c in range(N_CORES):
        b0, g = c // 2, c % 2
        xattn[b0, :, 512 * g: 512 * g + 512] = res.results[c]["slab"]

    wmap = {
        "wsum": np.ascontiguousarray(wsum),
        "wgt": np.ascontiguousarray(np.asarray(inp["L2_ffn_w_gate"], np.int32).T),
        "wut": np.ascontiguousarray(np.asarray(inp["L2_ffn_w_up"], np.int32).T),
        "wdt": np.ascontiguousarray(np.asarray(inp["L2_ffn_w_down"], np.int32).T),
    }
    maps = []
    for c in range(N_CORES):
        xa = xattn[:, c * S_LOC:(c + 1) * S_LOC, :]
        maps.append(dict(x_fm=xf[c],
                         xattn_fm=np.ascontiguousarray(xa.transpose(2, 0, 1).reshape(D, T)),
                         **wmap))
    res = _run("l2f", build_l2_ffn, maps)
    xf = [res.results[c]["y_fm"] for c in range(N_CORES)]

    xfull = _fm_unshard(xf)
    wo = np.asarray(inp["L3_attn_wo"], f32)
    maskT = np.zeros((3, 128, 512), f32)
    for kt in range(3):
        for kr in range(128):
            col = kt * 128 + kr
            for hi in range(4):
                lo = max(0, col - 256)
                if lo <= 127:
                    maskT[kt, kr, hi * 128 + lo: hi * 128 + min(128, col + 1)] = 1.0
    wmap = {
        "wq": np.asarray(inp["L3_attn_wq"], f32),
        "wk": np.asarray(inp["L3_attn_wk"], f32),
        "wv": np.asarray(inp["L3_attn_wv"], f32),
        "wo_sl": np.ascontiguousarray(wo.reshape(16, 64, 1024).transpose(1, 0, 2).reshape(64, 16384)),
        "bq_c": np.ascontiguousarray(np.asarray(inp["L3_attn_bq"], f32).reshape(8, 128).T),
        "bk_r": np.asarray(inp["L3_attn_bk"], f32).reshape(1, 1024).astype(bf16),
        "bv_r": np.asarray(inp["L3_attn_bv"], f32).reshape(1, 1024).astype(bf16),
        "bo_r": np.asarray(inp["L3_attn_bo"], f32).reshape(1, 1024).astype(bf16),
        "maskT": maskT.astype(bf16),
    }
    maps = []
    for c in range(N_CORES):
        x_ext = np.zeros((B, 1024, D), f32)
        valid = np.zeros((1, 4096), f32)
        lo = 512 * c - 128
        for b in range(B):
            s0 = max(0, lo)
            s1 = min(S, 512 * c + 640)
            x_ext[b, s0 - lo: s1 - lo] = xfull[b, s0:s1]
            valid[0, b * 1024 + (s0 - lo): b * 1024 + (s1 - lo)] = 1.0
        maps.append(dict(x_fm=xf[c],
                         x_ext=np.ascontiguousarray(x_ext.transpose(2, 0, 1).reshape(D, 4096)),
                         valid=valid.astype(bf16), **wmap))
    res = _run("l3a", build_l3_attn, maps)
    xf = [res.results[c]["y_fm"] for c in range(N_CORES)]

    wmap = {
        "gate_w": np.asarray(inp["L3_ffn_gate_w"], f32),
        "w1": np.asarray(inp["L3_ffn_w1"], f32),
        "w2": np.asarray(inp["L3_ffn_w2"], f32),
        "ident": np.eye(128, dtype=f32),
    }
    res = _run("l3f", build_l3_ffn, [dict(x_fm=xf[c], **wmap) for c in range(N_CORES)])
    xf = [res.results[c]["y_fm"] for c in range(N_CORES)]

    return _fm_unshard([np.asarray(s, f32) for s in xf])
